# version 07
# baseline (speedup 1.0000x reference)
import numpy as np
import ml_dtypes
import concourse.bass as bass
import concourse.mybir as mybir
from concourse.bass_utils import run_bass_kernel_spmd
from contextlib import ExitStack

F32 = mybir.dt.float32
BF16 = mybir.dt.bfloat16
A = mybir.AluOpType
AF = mybir.ActivationFunctionType
AX = mybir.AxisListType

S, D, H, DK, INNER = 2048, 1024, 16, 64, 4096
NTC = S // 128
NDC = D // 128
NIC = INNER // 512
NICD = INNER // 128
EPS_LN = 1e-5
EPS_RMS = 1.1920929e-07
SOFTCAP = 50.0

CONST_VALS = [EPS_LN, EPS_RMS, -8.0, -0.5e-5, 0.5e-5, -1e-5]


class Sem:
    def __init__(self, ctx, nc, name):
        self.h = ctx.enter_context(nc.semaphore(name))
        self.v = 0

    def inc(self, instr, n=1):
        instr.then_inc(self.h, n)
        self.v += n
        return instr


def _stair(nc, out_bf, src_f32, scale_ap, rO, rA, cb, fd, done_sem=None):
    v = nc.vector
    v.tensor_scalar(rO[:, 0:fd], src_f32, scale_ap, 128.5, A.mult, A.add)
    cur = rO
    for k in range(7, -1, -1):
        step = float(1 << k)
        v.tensor_scalar(cb[:, 0:fd], cur[:, 0:fd], step, step, A.is_ge, A.mult)
        nxt = rA if cur is rO else cur
        v.tensor_sub(nxt[:, 0:fd], cur[:, 0:fd], cb[:, 0:fd])
        cur = nxt
    ins = v.scalar_tensor_tensor(out_bf, rO[:, 0:fd], -128.0, cur[:, 0:fd],
                                 A.add, A.subtract)
    if done_sem is not None:
        done_sem.inc(ins, 1)


def build_nc(debug=(), stop_after=None, proj_mode=0):
    nc = bass.Bass("TRN2", target_bir_lowering=False, debug=False,
                   detect_race_conditions=False)

    x_in = nc.dram_tensor("x", [S, D], F32, kind="ExternalInput").ap()
    w_shapes = (("w_q", [D, D]), ("w_k", [D, D]), ("w_v", [D, D]),
                ("w_o", [D, D]), ("w_g", [INNER, D]), ("w_u", [INNER, D]),
                ("w_d", [D, INNER]))
    w_in = {nm: nc.dram_tensor(nm, sh, F32, kind="ExternalInput").ap()
            for nm, sh in w_shapes}
    ind_in = nc.dram_tensor("ind", [NDC, 128, 16], BF16, kind="ExternalInput").ap()
    indt_in = nc.dram_tensor("indt", [16, NDC, 128], F32, kind="ExternalInput").ap()
    ones_in = nc.dram_tensor("ones", [128, 128], F32, kind="ExternalInput").ap()
    cmask_in = nc.dram_tensor("cmask", [128, 128], BF16, kind="ExternalInput").ap()
    ident_in = nc.dram_tensor("ident", [128, 128], BF16, kind="ExternalInput").ap()
    consts_in = nc.dram_tensor("consts", [128, 8], F32, kind="ExternalInput").ap()
    out_hbm = nc.dram_tensor("out", [S, D], F32, kind="ExternalOutput").ap()
    dbg = {}
    for name, shape, dt in debug:
        dbg[name] = nc.dram_tensor("dbg_" + name, shape, dt,
                                   kind="ExternalOutput").ap()

    s_w = {nm: nc.dram_tensor("s_" + nm, sh, BF16, kind="Internal").ap()
           for nm, sh in w_shapes}
    s_wT = {"w_g": nc.dram_tensor("s_wT_g", [D, INNER], BF16, kind="Internal").ap(),
            "w_u": nc.dram_tensor("s_wT_u", [D, INNER], BF16, kind="Internal").ap(),
            "w_d": nc.dram_tensor("s_wT_d", [INNER, D], BF16, kind="Internal").ap()}
    s_xn = nc.dram_tensor("s_xn", [S, D], BF16, kind="Internal").ap()
    s_xq = nc.dram_tensor("s_xq", [S, D], BF16, kind="Internal").ap()
    s_y = nc.dram_tensor("s_y", [S, D], BF16, kind="Internal").ap()
    s_h = nc.dram_tensor("s_h", [S, INNER], F32, kind="Internal").ap()
    s_hq = nc.dram_tensor("s_hq", [S, INNER], BF16, kind="Internal").ap()
    s_res = nc.dram_tensor("s_res", [S, D], F32, kind="Internal").ap()

    ctx = ExitStack()
    ctxLN = ExitStack()
    ctxPJ = ExitStack()
    ctxU5 = ExitStack()
    ctxQ = ExitStack()
    ctxB = ExitStack()
    ctxC = ExitStack()
    ctxF = ExitStack()
    ctxG = ExitStack()
    sb = lambda st, name, shape, dt: st.enter_context(nc.sbuf_tensor(name, shape, dt))
    sbr = lambda st, name, shape, dt: st.enter_context(nc.sbuf_tensor(name, shape, dt, side="right"))

    cst = sb(ctx, "cst", [128, 8], F32)
    ind = sb(ctx, "indb", [128, NDC, 16], BF16)
    ones = sb(ctx, "onesb", [128, 128], F32)
    cmask = sb(ctx, "cmaskb", [128, 128], BF16)
    identb = sb(ctx, "identb", [128, 128], BF16)
    stats = sb(ctx, "stats", [128, NTC, 10], F32)
    th3 = sb(ctx, "th3", [128, 8], F32)
    rmax = sb(ctx, "rmax", [128, 2, NTC], F32)
    sc2 = sb(ctx, "sc2", [128, NTC, 4], F32)

    for i, vv in enumerate(CONST_VALS):
        nc.const_aps.register(F32, vv, cst[:, i:i + 1])

    wstg = sb(ctxU5, "wstg", [128, 2, D], F32)
    accA = sb(ctxU5, "accA", [128, 3, 32], F32)
    sgA = sb(ctxU5, "sgA", [128, 2, D], BF16)
    sgB = sb(ctxU5, "sgB", [128, 2, D], BF16)
    w3o = sb(ctxU5, "w3o", [128, 2, D], BF16)
    wte = sb(ctxU5, "wte", [128, 2, D], BF16)

    xld = sb(ctxLN, "xld", [128, 2, D], F32)
    xc2 = sb(ctxLN, "xc2", [128, 2, D], F32)
    xsq = sb(ctxLN, "xsq", [128, D], F32)
    stgn = sb(ctxLN, "stgn", [128, 2, D], BF16)
    stgq = sb(ctxLN, "stgq", [128, 2, D], BF16)
    rO = sb(ctxLN, "rO", [128, D], F32)
    rA = sb(ctxLN, "rA", [128, D], F32)
    cbf = sb(ctxLN, "cbf", [128, D], F32)

    xn_fm = sbr(ctxPJ, "xn_fm", [128, NDC, S], BF16)
    indt = sbr(ctxPJ, "indtb", [16, NDC, 128], F32)
    wT = sbr(ctxPJ, "wT", [128, NDC, D], BF16)
    qsq = sbr(ctxPJ, "qsq", [128, 2, 512], BF16)
    a_q = sbr(ctxPJ, "a_q", [16, S], F32)
    a_ln = sbr(ctxPJ, "a_ln", [16, 512], F32)

    psU1 = ctx.enter_context(nc.psum_tensor("psU1", [128, 512], F32))
    psU2 = ctx.enter_context(nc.psum_tensor("psU2", [128, 512], F32))
    psSCa = ctx.enter_context(nc.psum_tensor("psSCa", [128, 1024], F32))
    psSCb = ctx.enter_context(nc.psum_tensor("psSCb", [128, 1024], F32))
    psY = ctx.enter_context(nc.psum_tensor("psY", [128, 512], F32))
    ctxPSS = ExitStack()
    psSS = ctxPSS.enter_context(nc.psum_tensor("psSS", [16, 512], F32))
    psT = [None]

    all_sems = []

    def sm(name):
        x = Sem(ctx, nc, name)
        all_sems.append(x)
        return x
    dX = [sm("dXa"), sm("dXb")]
    dXN = [sm("dXNa"), sm("dXNb")]
    dXNL, dXQL = sm("dXNL"), sm("dXQL")
    dAW, dWT = sm("dAW"), sm("dWT")
    sLN1, sLN2, sLN3, sLN4, sLN5 = sm("sLN1"), sm("sLN2"), sm("sLN3"), sm("sLN4"), sm("sLN5")
    sA1, sA2, sA3, sA4 = sm("sA1"), sm("sA2"), sm("sA3"), sm("sA4")
    mPJ, sSQ, sCP, mSS, sA, mBC, eRMS = sm("mPJ"), sm("sSQ"), sm("sCP"), sm("mSS"), sm("sA"), sm("mBC"), sm("eRMS")
    dF1 = [sm("dF1a"), sm("dF1b")]
    dF2 = [sm("dF2a"), sm("dF2b")]
    dW3 = [sm("dW3a"), sm("dW3b")]
    sAB, mMean, eTH, sSG, eW3 = sm("sAB"), sm("mMean"), sm("eTH"), sm("sSG"), sm("eW3")
    mQK, sTNH, sEXP, eMSK, mPV, eREC, sYN = sm("mQK"), sm("sTNH"), sm("sEXP"), sm("eMSK"), sm("mPV"), sm("eREC"), sm("sYN")
    dGU = [sm("dGUa"), sm("dGUb")]
    dH = [sm("dHa"), sm("dHb")]
    mGU, sSIL, eH = sm("mGU"), sm("sSIL"), sm("eH")
    eH2, eAS, eAS2, eH3, eW3b, eRM = sm("eH2"), sm("eAS"), sm("eAS2"), sm("eH3"), sm("eW3b"), sm("eRM")
    eCL, eRC = sm("eCL"), sm("eRC")
    mTW, eWT, mTR, eTR = sm("mTW"), sm("eWT"), sm("mTR"), sm("eTR")
    dY, dYL, dWO, mO, eRS = sm("dY"), sm("dYL"), sm("dWO"), sm("mO"), sm("eRS")
    dRS = [sm("dRSa"), sm("dRSb")]
    dXR = [sm("dXRa"), sm("dXRb")]
    dHL, eSTR, dHQL, dWD, mDN, eOUT = sm("dHL"), sm("eSTR"), sm("dHQL"), sm("dWD"), sm("mDN"), sm("eOUT")
    dHQq = [sm("dHQ0"), sm("dHQ1"), sm("dHQ2"), sm("dHQ3")]
    dRL = [sm("dRLa"), sm("dRLb")]
    dOUT = [sm("dOUTa"), sm("dOUTb")]

    blk = ctx.enter_context(nc.Block())

    class _Done(Exception):
        pass

    def _body(_sync):
        T, V, SC, G, SY = nc.tensor, nc.vector, nc.scalar, nc.gpsimd, nc.sync

        def probe_stop(tag):
            if stop_after == tag:
                for sx in all_sems:
                    if sx.v:
                        nc.gpsimd.wait_ge(sx.h, sx.v)
                if "stats" in dbg:
                    dAW.inc(nc.gpsimd.dma_start(out=dbg["stats"], in_=stats[:]), 16)
                if "s_xn" in dbg:
                    dAW.inc(nc.gpsimd.dma_start(out=dbg["s_xn"], in_=s_xn), 16)
                if "s_xq" in dbg:
                    dAW.inc(nc.gpsimd.dma_start(out=dbg["s_xq"], in_=s_xq), 16)
                nc.gpsimd.wait_ge(dAW.h, dAW.v)
                raise _Done

        dAW.inc(G.dma_start(out=cst[:], in_=consts_in), 16)
        for dc in range(NDC):
            dAW.inc(G.dma_start(out=ind[:, dc, :], in_=ind_in[dc]), 16)
        dAW.inc(G.dma_start(out=ones[:], in_=ones_in), 16)
        dAW.inc(G.dma_start(out=cmask[:], in_=cmask_in), 16)
        dAW.inc(G.dma_start(out=identb[:], in_=ident_in), 16)
        for nm in ("w_q", "w_k", "w_v", "w_o"):
            dAW.inc(G.dma_start(out=s_w[nm], in_=w_in[nm]), 16)

        WMETA = {"g": ("w_g", 0), "u": ("w_u", 1), "d": ("w_d", 2)}

        def u5_chunk_ap(nm, chi):
            if nm == "w_d":
                r, c = chi // 4, chi % 4
                return (slice(r * 128, (r + 1) * 128),
                        slice(c * 1024, (c + 1) * 1024))
            return (slice(chi * 128, (chi + 1) * 128), slice(0, D))

        def u5_s1(m, lo, hi):
            nm, widx = WMETA[m]
            for chi in range(lo, hi):
                par = chi % 2
                SY.wait_ge(sAB.h, max(0, sAB.v - 1))
                SY.wait_ge(sSG.h, sSG.v)
                dF1[par].inc(SY.dma_start(out=wstg[:, par, :],
                                     in_=w_in[nm][u5_chunk_ap(nm, chi)]), 16)
                SC.wait_ge(dF1[par].h, dF1[par].v)
                sAB.inc(SC.activation(sgA[:, par, :], wstg[:, par, :], AF.Abs,
                                      accum_out=accA[:, widx, chi:chi + 1]), 1)

        def u5_fin(m):
            _, col = WMETA[m]
            T.wait_ge(sAB.h, sAB.v)
            T.wait_ge(sCP.h, sCP.v)
            T.wait_ge(eW3b.h, eW3b.v)
            mMean.inc(T.matmul(psU1[:, 0:32], ones[:], accA[:, col, :],
                               start=True, stop=True), 1)
            V.wait_ge(mMean.h, mMean.v)
            eW3b.inc(V.tensor_reduce(th3[:, 4:5], psU1[:, 0:32], axis=AX.X,
                                     op=A.add), 1)
            SC.wait_ge(eW3b.h, eW3b.v)
            SC.activation(th3[:, 5:6], th3[:, 4:5], AF.Relu,
                          scale=0.5 / (INNER * D), bias=-0.5e-5)
            eTH.inc(SC.activation(th3[:, col:col + 1], th3[:, 5:6], AF.Identity,
                                  bias=0.5e-5), 1)

        def u5T_view(nm, chi):
            if nm == "w_d":
                r, c = chi // 4, chi % 4
                reg = s_wT[nm][c * 1024:(c + 1) * 1024, r * 128:(r + 1) * 128]
            else:
                reg = s_wT[nm][:, chi * 128:(chi + 1) * 128]
            return reg.rearrange("(b p) w -> p b w", b=8)

        w3_ready = {}

        def u5_s2a(m, lo, hi):
            nm, col = WMETA[m]
            for chi in range(lo, hi):
                par = chi % 2
                SY.wait_ge(sSG.h, max(0, sSG.v - 1))
                SY.wait_ge(sAB.h, sAB.v)
                dF2[par].inc(SY.dma_start(out=wstg[:, par, :],
                                     in_=w_in[nm][u5_chunk_ap(nm, chi)]), 16)
                SC.wait_ge(dF2[par].h, dF2[par].v)
                SC.wait_ge(eTH.h, eTH.v)
                SC.wait_ge(eW3.h, max(0, eW3.v - 1))
                SC.activation(sgA[:, par, :], wstg[:, par, :], AF.Sign,
                              scale=-1.0, bias=th3[:, col:col + 1])
                sSG.inc(SC.activation(sgB[:, par, :], wstg[:, par, :], AF.Sign,
                                      bias=th3[:, col:col + 1]), 1)
                V.wait_ge(sSG.h, sSG.v)
                V.wait_ge(dW3[par].h, dW3[par].v)
                eW3.inc(V.tensor_sub(w3o[:, par, :], sgB[:, par, :],
                                     sgA[:, par, :]), 1)
                w3_ready[(m, chi)] = eW3.v

        def u5_s2b(m, lo, hi):
            nm, col = WMETA[m]
            for chi in range(lo, hi):
                par = chi % 2
                T.wait_ge(eW3.h, w3_ready[(m, chi)])
                T.wait_ge(eWT.h, eWT.v)
                for j in range(NDC):
                    ins = T.matmul(psT[0][:, j * 128:(j + 1) * 128],
                                   w3o[:, par, j * 128:(j + 1) * 128],
                                   identb[:], start=(j == 0), stop=(j == NDC - 1),
                                   is_transpose=True)
                mTW.inc(ins, 1)
                V.wait_ge(mTW.h, mTW.v)
                eWT.inc(V.tensor_copy(wte[:, par, :], psT[0][:]), 1)
                G.wait_ge(eWT.h, eWT.v)
                dW3[par].inc(G.dma_start(
                    out=u5T_view(nm, chi),
                    in_=wte[:, par, :].rearrange("p (b w) -> p b w", b=8)), 16)

        u5_plan = []
        for m in ("g", "u", "d"):
            for j in range(16):
                u5_plan.append(("s1", m, j * 2, (j + 1) * 2))
        s2a = [("s2a", m, j, j + 1) for m in ("g", "u", "d") for j in range(32)]
        s2b = [("s2b", m, j, j + 1) for m in ("g", "u", "d") for j in range(32)]
        for i in range(len(s2a)):
            u5_plan.append(s2a[i])
            if i >= 1:
                u5_plan.append(s2b[i - 1])
        u5_plan.append(s2b[-1])
        u5_pos = [0]

        def u5_pieces(n):
            for _ in range(n):
                if u5_pos[0] >= len(u5_plan):
                    return
                e = u5_plan[u5_pos[0]]
                u5_pos[0] += 1
                if e[0] == "s1":
                    u5_s1(e[1], e[2], e[3])
                elif e[0] == "fin":
                    u5_fin(e[1])
                elif e[0] == "s2a":
                    u5_s2a(e[1], e[2], e[3])
                else:
                    u5_s2b(e[1], e[2], e[3])


        for tc in range(NTC):
            par = tc % 2
            row = slice(tc * 128, (tc + 1) * 128)
            if tc >= 2:
                SY.wait_ge(sLN5.h, tc - 1)
            dX[par].inc(SY.dma_start(out=xld[:, par, :], in_=x_in[row, :]), 16)
            V.wait_ge(dX[par].h, dX[par].v)
            sLN1.inc(V.tensor_reduce(stats[:, tc, 0:1], xld[:, par, :],
                                     axis=AX.X, op=A.add, negate=True), 1)
            SC.wait_ge(sLN1.h, sLN1.v)
            sA1.inc(SC.activation(stats[:, tc, 1:2], stats[:, tc, 0:1],
                                  AF.Identity, scale=1.0 / D), 1)
            V.wait_ge(sA1.h, sA1.v)
            if tc >= 2:
                V.wait_ge(sLN5.h, tc - 1)
            sA2.inc(V.tensor_scalar(xc2[:, par, :], xld[:, par, :],
                                    stats[:, tc, 1:2], None, A.add), 1)
            SC.wait_ge(sA2.h, sA2.v)
            sLN2.inc(SC.activation(xsq[:], xc2[:, par, :], AF.Square), 1)
            V.wait_ge(sLN2.h, sLN2.v)
            V.tensor_reduce(stats[:, tc, 2:3], xsq[:], axis=AX.X, op=A.add)
            sLN3.inc(V.tensor_reduce(stats[:, tc, 3:4], xc2[:, par, :],
                                     axis=AX.X, op=A.max,
                                     apply_absolute_value=True), 1)
            SC.wait_ge(sLN3.h, sLN3.v)
            SC.activation(stats[:, tc, 4:5], stats[:, tc, 2:3], AF.Ln,
                          scale=1.0 / D, bias=EPS_LN)
            sLN4.inc(SC.activation(stats[:, tc, 5:6], stats[:, tc, 4:5],
                                   AF.Exp, scale=-0.5), 1)
            V.wait_ge(sLN4.h, sLN4.v)
            sA3.inc(V.tensor_scalar(stats[:, tc, 6:7], stats[:, tc, 3:4],
                                    stats[:, tc, 5:6], 1e-5, A.mult, A.max), 1)
            SC.wait_ge(sA3.h, sA3.v)
            SC.activation(stats[:, tc, 4:5], stats[:, tc, 6:7], AF.Ln)
            sA4.inc(SC.activation(stats[:, tc, 7:8], stats[:, tc, 4:5],
                                  AF.Exp, scale=-1.0), 1)
            V.wait_ge(sA4.h, sA4.v)
            V.tensor_scalar(stats[:, tc, 8:9], stats[:, tc, 7:8],
                            stats[:, tc, 5:6], 127.0, A.mult, A.mult)
            V.tensor_scalar(stats[:, tc, 9:10], stats[:, tc, 6:7], 1.0 / 127.0,
                            None, A.mult)
            if tc >= 2:
                V.wait_ge(dXN[par].h, dXN[par].v)
            V.tensor_scalar(stgn[:, par, :], xc2[:, par, :], stats[:, tc, 5:6],
                            None, A.mult)
            _stair(nc, stgq[:, par, :], xc2[:, par, :], stats[:, tc, 8:9],
                   rO, rA, cbf, D, done_sem=sLN5)
            G.wait_ge(sLN5.h, sLN5.v)
            dXN[par].inc(G.dma_start(out=s_xn[row, :], in_=stgn[:, par, :]), 16)
            dXN[par].inc(G.dma_start(out=s_xq[row, :], in_=stgq[:, par, :]), 16)
            u5_pieces(3)

        for m in ("g", "u", "d"):
            u5_fin(m)
        T.wait_ge(eW3b.h, eW3b.v)

        probe_stop("u0")
        dAW.inc(G.dma_start(out=indt[0:16, :, :], in_=indt_in), 16)
        SY.wait_ge(dXN[0].h, dXN[0].v)
        SY.wait_ge(dXN[1].h, dXN[1].v)
        for dc in range(NDC):
            dXNL.inc(SY.dma_start(out=xn_fm[:, dc, :],
                                  in_=s_xn[:, dc * 128:(dc + 1) * 128],
                                  transpose=True), 16)
        ctxLN.close()

        q_fm = sb(ctxQ, "q_fm", [128, NDC, S], BF16)
        k_fm = sb(ctxQ, "k_fm", [128, NDC, S], BF16)
        v_ext = sb(ctxQ, "v_ext", [128, NTC, 16, 65], BF16)
        V.memset(v_ext[:], 1.0)

        evac_gate = {id(psU1): (0, 0), id(psU2): (0, 0)}

        def proj_qk(wscr, dst_fm):
            SY.wait_ge(dAW.h, dAW.v)
            SY.wait_ge(mPJ.h, mPJ.v)
            for dc in range(NDC):
                dWT.inc(SY.dma_start(out=wT[:, dc, :],
                                     in_=wscr[:, dc * 128:(dc + 1) * 128],
                                     transpose=True), 16)
            probe_stop("uqw")
            for ts in range(4):
                tsl = slice(ts * 512, (ts + 1) * 512)
                for oc in range(NDC):
                    ps = psU1 if oc % 2 == 0 else psU2
                    g1, g2 = evac_gate[id(ps)]
                    T.wait_ge(sSQ.h, g1)
                    T.wait_ge(sCP.h, g2)
                    T.wait_ge(dWT.h, dWT.v)
                    T.wait_ge(dXNL.h, dXNL.v)
                    for dc in range(NDC):
                        ins = T.matmul(ps[:], wT[:, dc, oc * 128:(oc + 1) * 128],
                                       xn_fm[:, dc, tsl], start=(dc == 0),
                                       stop=(dc == NDC - 1))
                    mPJ.inc(ins, 1)
                    par2 = oc % 2
                    SC.wait_ge(mPJ.h, mPJ.v)
                    sSQ.inc(SC.activation(qsq[:, par2, :], ps[:], AF.Square), 1)
                    V.wait_ge(sSQ.h, sSQ.v)
                    sCP.inc(V.tensor_copy(dst_fm[:, oc, tsl], ps[:]), 1)
                    evac_gate[id(ps)] = (sSQ.v, sCP.v)
                    probe_stop("uq0")
                    if proj_mode >= 2:
                        continue
                    T.wait_ge(sSQ.h, sSQ.v)
                    if oc == 0:
                        T.wait_ge(sA.h, sA.v)
                    ins2 = T.matmul(psSS[:], ind[:, oc, :], qsq[:, par2, :],
                                    start=(oc == 0), stop=(oc == NDC - 1))
                if proj_mode >= 2:
                    continue
                mSS.inc(ins2, 1)
                SC.wait_ge(mSS.h, mSS.v)
                SC.activation(a_ln[:], psSS[:], AF.Ln, scale=1.0 / DK,
                              bias=EPS_RMS)
                sA.inc(SC.activation(a_q[:, tsl], a_ln[:], AF.Exp, scale=-0.5), 1)
                if proj_mode >= 1:
                    continue
                for oc in range(NDC):
                    T.wait_ge(sA.h, sA.v)
                    T.wait_ge(eRMS.h, eRMS.v)
                    mBC.inc(T.matmul(psY[:], indt[0:16, oc, :], a_q[0:16, tsl],
                                     start=True, stop=True), 1)
                    V.wait_ge(mBC.h, mBC.v)
                    eRMS.inc(V.tensor_mul(dst_fm[:, oc, tsl], dst_fm[:, oc, tsl],
                                          psY[:]), 1)

        proj_qk(s_w["w_q"], q_fm)
        probe_stop("uq")
        proj_qk(s_w["w_k"], k_fm)

        SY.wait_ge(mPJ.h, mPJ.v)
        for dc in range(NDC):
            dWT.inc(SY.dma_start(out=wT[:, dc, :],
                                 in_=s_w["w_v"][:, dc * 128:(dc + 1) * 128],
                                 transpose=True), 16)
        for tc in range(NTC):
            for vs in range(2):
                ps = psU1 if (tc * 2 + vs) % 2 == 0 else psU2
                g1, g2 = evac_gate[id(ps)]
                T.wait_ge(sCP.h, g2)
                T.wait_ge(dWT.h, dWT.v)
                for dc in range(NDC):
                    ins = T.matmul(ps[:], xn_fm[:, dc, tc * 128:(tc + 1) * 128],
                                   wT[:, dc, vs * 512:(vs + 1) * 512],
                                   start=(dc == 0), stop=(dc == NDC - 1))
                mPJ.inc(ins, 1)
                V.wait_ge(mPJ.h, mPJ.v)
                sCP.inc(V.tensor_copy(
                    v_ext[:, tc, vs * 8:(vs + 1) * 8, 0:64],
                    ps[:].rearrange("p (h f) -> p h f", h=8)), 1)
                evac_gate[id(ps)] = (sSQ.v, sCP.v)

        ctxPJ.close()
        ctxPSS.close()
        psT[0] = ctx.enter_context(nc.psum_tensor("psT", [128, 1024], BF16))
        probe_stop("u3")
        y_tm = sbr(ctxB, "y_tm", [128, NTC, D], BF16)
        t1b = sbr(ctxB, "t1b", [128, 2, 1024], F32)
        p_sb = sbr(ctxB, "p_sb", [128, 2, 1024], BF16)
        den = sbr(ctxB, "den", [128, 4], F32)

        T.wait_ge(eRMS.h, eRMS.v)
        T.wait_ge(sCP.h, sCP.v)
        for dnm, dbuf in (("q_fm", q_fm), ("k_fm", k_fm), ("v_ext", v_ext)):
            if dnm in dbg:
                G.wait_ge(eRMS.h, eRMS.v)
                G.wait_ge(sCP.h, sCP.v)
                dAW.inc(G.dma_start(out=dbg[dnm], in_=dbuf[:]), 16)

        pair_idx = [0]
        sc_gate = [0, 0]
        p_gate = [0, 0]

        for h in range(H):
            oc = h // 2
            pr = (h % 2) * 64
            for qt in range(4):
                qsl = slice(qt * 512, (qt + 1) * 512)
                T.wait_ge(sYN.h, h * 4 + qt)
                nr = 4 * qt + 4
                for j in range(nr // 2):
                    r0, r1 = 2 * j, 2 * j + 1
                    par = pair_idx[0] % 2
                    ps = psSCa if par == 0 else psSCb
                    T.wait_ge(sTNH.h, sc_gate[par])
                    for rr, half in ((r0, 0), (r1, 1)):
                        ins = T.matmul(ps[:, half * 512:(half + 1) * 512],
                                       k_fm[pr:pr + 64, oc, rr * 128:(rr + 1) * 128],
                                       q_fm[pr:pr + 64, oc, qsl],
                                       start=True, stop=True)
                    mQK.inc(ins, 1)
                    SC.wait_ge(mQK.h, mQK.v)
                    SC.wait_ge(mPV.h, p_gate[par])
                    sTNH.inc(SC.activation(t1b[:, par, :], ps[:], AF.Tanh,
                                           scale=1.0 / (8.0 * SOFTCAP)), 1)
                    sc_gate[par] = sTNH.v
                    sEXP.inc(SC.activation(p_sb[:, par, :], t1b[:, par, :],
                                           AF.Exp, scale=SOFTCAP, bias=-8.0), 1)
                    nmask = 0
                    for rr, half in ((r0, 0), (r1, 1)):
                        if 4 * qt <= rr:
                            V.wait_ge(sEXP.h, sEXP.v)
                            sl = half * 512 + (rr - 4 * qt) * 128
                            eMSK.inc(V.tensor_mul(p_sb[:, par, sl:sl + 128],
                                                  p_sb[:, par, sl:sl + 128],
                                                  cmask[:]), 1)
                            nmask += 1
                    T.wait_ge(sEXP.h, sEXP.v)
                    if nmask:
                        T.wait_ge(eMSK.h, eMSK.v)
                    for rr, half in ((r0, 0), (r1, 1)):
                        for cl in range(4):
                            c = 4 * qt + cl
                            if c < rr:
                                continue
                            ins = T.matmul(psY[:, cl * 128:cl * 128 + 65],
                                           p_sb[:, par, half * 512 + cl * 128:
                                                half * 512 + (cl + 1) * 128],
                                           v_ext[:, rr, h, :],
                                           start=(rr == 0 and cl == 0),
                                           stop=(rr == 4 * qt + 3 and cl == 3))
                    mPV.inc(ins, 1)
                    p_gate[par] = mPV.v
                    pair_idx[0] += 1
                V.wait_ge(mPV.h, mPV.v)
                for cl in range(4):
                    ins = V.reciprocal(den[:, cl:cl + 1],
                                       psY[:, cl * 128 + 64:cl * 128 + 65])
                eREC.inc(ins, 1)
                SC.wait_ge(eREC.h, eREC.v)
                for cl in range(4):
                    ins = SC.activation(y_tm[:, 4 * qt + cl, h * 64:(h + 1) * 64],
                                        psY[:, cl * 128:cl * 128 + 64],
                                        AF.Identity, scale=den[:, cl:cl + 1])
                sYN.inc(ins, 1)
                u5_pieces(3)
        u5_pieces(100)

        probe_stop("u4")
        ctxQ.close()
        ctxU5.close()

        y_fm = sb(ctxC, "y_fm", [128, NDC, S], BF16)
        woT = sb(ctxC, "woT", [128, NDC, D], BF16)
        x_re = sb(ctxC, "x_re", [128, 2, D], F32)
        res_sb = sb(ctxC, "res_sb", [128, 2, D], F32)

        G.wait_ge(sYN.h, sYN.v)
        if "y_tm" in dbg:
            dAW.inc(G.dma_start(out=dbg["y_tm"], in_=y_tm[:]), 16)
        for tc in range(NTC):
            dY.inc(G.dma_start(out=s_y[tc * 128:(tc + 1) * 128, :],
                               in_=y_tm[:, tc, :]), 16)
        SY.wait_ge(dY.h, dY.v)
        SY.wait_ge(dW3[0].h, dW3[0].v)
        SY.wait_ge(dW3[1].h, dW3[1].v)
        for dc in range(NDC):
            dYL.inc(SY.dma_start(out=y_fm[:, dc, :],
                                 in_=s_y[:, dc * 128:(dc + 1) * 128],
                                 transpose=True), 16)
            dWO.inc(SY.dma_start(out=woT[:, dc, :],
                                 in_=s_w["w_o"][:, dc * 128:(dc + 1) * 128],
                                 transpose=True), 16)
        T.wait_ge(eTH.h, eTH.v)
        res_gate = {id(psU1): 0, id(psU2): 0}
        for tc in range(NTC):
            par = tc % 2
            row = slice(tc * 128, (tc + 1) * 128)
            if tc >= 2:
                SY.wait_ge(eRS.h, 2 * (tc - 1))
            dXR[par].inc(SY.dma_start(out=x_re[:, par, :], in_=x_in[row, :]), 16)
            for os_ in range(2):
                ps = psU1 if os_ == 0 else psU2
                T.wait_ge(eRS.h, res_gate[id(ps)])
                T.wait_ge(dYL.h, dYL.v)
                T.wait_ge(dWO.h, dWO.v)
                for dc in range(NDC):
                    ins = T.matmul(ps[:], y_fm[:, dc, tc * 128:(tc + 1) * 128],
                                   woT[:, dc, os_ * 512:(os_ + 1) * 512],
                                   start=(dc == 0), stop=(dc == NDC - 1))
                mO.inc(ins, 1)
                V.wait_ge(mO.h, mO.v)
                V.wait_ge(dXR[par].h, dXR[par].v)
                if tc >= 2 and os_ == 0:
                    V.wait_ge(dRS[par].h, dRS[par].v)
                eRS.inc(V.tensor_add(res_sb[:, par, os_ * 512:(os_ + 1) * 512],
                                     ps[:],
                                     x_re[:, par, os_ * 512:(os_ + 1) * 512]), 1)
                res_gate[id(ps)] = eRS.v
            G.wait_ge(eRS.h, eRS.v)
            dRS[par].inc(G.dma_start(out=s_res[row, :], in_=res_sb[:, par, :]), 16)
        ctxB.close()
        ctxC.close()

        probe_stop("u8")
        xqg = sb(ctxF, "xqg", [128, 2, NDC, 512], BF16)
        gusl = sb(ctxF, "gusl", [128, 2, 2, NDC, 512], BF16)
        gate_sb = sb(ctxF, "gate_sb", [128, 512], F32)
        h_sb = sb(ctxF, "h_sb", [128, 2, 512], F32)
        hload = sb(ctxF, "hload", [128, 2, INNER], F32)
        r7O = sb(ctxF, "r7O", [128, INNER // 4], F32)
        r7A = sb(ctxF, "r7A", [128, INNER // 4], F32)
        c7 = psSCb
        hqq = sb(ctxF, "hqq", [128, 2, 1024], BF16)
        wdT = sb(ctxF, "wdT", [128, NICD, D], BF16)
        hqT = sb(ctxF, "hqT", [128, 4, NICD, 128], BF16)
        res_ld = sb(ctxF, "res_ld", [128, 1, D], F32)
        out_sb = sb(ctxF, "out_sb", [128, 1, D], F32)

        V.wait_ge(eTH.h, eTH.v)
        for tc in range(NTC):
            V.tensor_scalar(sc2[:, tc, 2:3], stats[:, tc, 9:10],
                            th3[:, 0:1], None, A.mult)
            V.tensor_scalar(sc2[:, tc, 3:4], stats[:, tc, 9:10],
                            th3[:, 1:2], None, A.mult)
        SY.wait_ge(sYN.h, 4 * H)
        SY.wait_ge(dY.h, dY.v)
        SY.wait_ge(dRS[0].h, dRS[0].v)
        SY.wait_ge(dRS[1].h, dRS[1].v)
        SY.wait_ge(dW3[0].h, dW3[0].v)
        SY.wait_ge(dW3[1].h, dW3[1].v)
        T.wait_ge(eRS.h, eRS.v)
        T.wait_ge(sTNH.h, sTNH.v)
        gu_gate = [0, 0]
        xq_gate = [0, 0]
        psg = [0, 0]
        dn_gate = [0, 0]
        psDN = (psSCa[:, 0:512], psSCa[:, 512:1024])
        dXQg = [dXQL, dYL]

        def drain_stair(n=1):
            for _ in range(n):
                if stair_q:
                    stair_q.pop(0)()

        def u6_group(grp, drain=False):
            xbuf = grp % 2
            SY.wait_ge(mGU.h, xq_gate[xbuf])
            for dc in range(NDC):
                dXQg[xbuf].inc(SY.dma_start(
                    out=xqg[:, xbuf, dc, :],
                    in_=s_xq[grp * 512:(grp + 1) * 512,
                             dc * 128:(dc + 1) * 128],
                    transpose=True), 16)
            for ic in range(NIC):
                buf = (grp * NIC + ic) % 2
                SY.wait_ge(mGU.h, gu_gate[buf])
                for w, nm in ((0, "w_g"), (1, "w_u")):
                    dGU[buf].inc(SY.dma_start(
                        out=gusl[:, w, buf, :, :],
                        in_=s_wT[nm][:, ic * 512:(ic + 1) * 512]
                            .rearrange("(b p) c -> p b c", b=NDC)), 16)
                for tl in range(4):
                    tc = grp * 4 + tl
                    T.wait_ge(dGU[buf].h, dGU[buf].v)
                    T.wait_ge(dXQg[xbuf].h, dXQg[xbuf].v)
                    T.wait_ge(sSIL.h, psg[0])
                    for dc in range(NDC):
                        T.matmul(psU1[:],
                                 xqg[:, xbuf, dc, tl * 128:(tl + 1) * 128],
                                 gusl[:, 0, buf, dc, :], start=(dc == 0),
                                 stop=(dc == NDC - 1))
                    T.wait_ge(eH.h, psg[1])
                    for dc in range(NDC):
                        ins = T.matmul(psU2[:],
                                       xqg[:, xbuf, dc, tl * 128:(tl + 1) * 128],
                                       gusl[:, 1, buf, dc, :], start=(dc == 0),
                                       stop=(dc == NDC - 1))
                    mGU.inc(ins, 1)
                    SC.wait_ge(mGU.h, mGU.v)
                    SC.wait_ge(eH.h, eH.v)
                    sSIL.inc(SC.activation(gate_sb[:], psU1[:], AF.Silu,
                                           scale=sc2[:, tc, 2:3]), 1)
                    psg[0] = sSIL.v
                    hpar = (grp * 32 + ic * 4 + tl) % 2
                    V.wait_ge(sSIL.h, sSIL.v)
                    V.wait_ge(dH[hpar].h, dH[hpar].v)
                    eH.inc(V.scalar_tensor_tensor(h_sb[:, hpar, :], psU2[:],
                                                  sc2[:, tc, 3:4], gate_sb[:],
                                                  A.mult, A.mult), 1)
                    psg[1] = eH.v
                    G.wait_ge(eH.h, eH.v)
                    dH[hpar].inc(G.dma_start(
                        out=s_h[tc * 128:(tc + 1) * 128,
                                ic * 512:(ic + 1) * 512],
                        in_=h_sb[:, hpar, :]), 16)
                gu_gate[buf] = mGU.v
                if drain:
                    drain_stair(3)
            xq_gate[xbuf] = mGU.v

        def mk_stair_pieces(grp):
            snap0, snap1 = dH[0].v, dH[1].v
            qw = INNER // 4

            def pro(tc):
                row = slice(tc * 128, (tc + 1) * 128)
                SC.wait_ge(dH[0].h, snap0)
                SC.wait_ge(dH[1].h, snap1)
                SC.wait_ge(eSTR.h, max(0, eSTR.v - 4))
                dHL.inc(SC.dma_start(out=hload[:, tc % 2, :],
                                     in_=s_h[row, :]), 16)

            def piece_a(tc, first):
                pro(tc)
                V.wait_ge(dHL.h, dHL.v)
                eRM.inc(V.tensor_reduce(rmax[:, 0, tc:tc + 1],
                                        hload[:, tc % 2, :],
                                        axis=AX.X, op=A.max,
                                        apply_absolute_value=True), 1)
                SC.wait_ge(eRM.h, eRM.v)
                SC.activation(rmax[:, 1, tc:tc + 1], rmax[:, 0, tc:tc + 1],
                              AF.Relu, bias=-1e-5)
                eCL.inc(SC.activation(rmax[:, 0, tc:tc + 1],
                                      rmax[:, 1, tc:tc + 1], AF.Identity,
                                      bias=EPS_LN), 1)
                V.wait_ge(eCL.h, eCL.v)
                eRC.inc(V.reciprocal(rmax[:, 1, tc:tc + 1],
                                     rmax[:, 0, tc:tc + 1]), 1)
                V.tensor_scalar(sc2[:, tc, 0:1], rmax[:, 0, tc:tc + 1],
                                th3[:, 2:3], 1.0 / 127.0, A.mult,
                                A.mult)
                SC.wait_ge(eRC.h, eRC.v)
                eAS.inc(SC.activation(sc2[:, tc, 1:2], rmax[:, 1, tc:tc + 1],
                                      AF.Identity, scale=127.0), 1)
                V.wait_ge(eAS.h, eAS.v)
                if tc >= 1:
                    pq = (tc - 1) // 4
                    V.wait_ge(dHQq[pq].h, dHQq[pq].v)

            def quarter(tc, qi):
                tl = tc % 4
                qb = (tc * 4 + qi) % 2
                V.wait_ge(mTR.h, max(0, mTR.v - 1))
                _stair(nc, hqq[:, qb, :], hload[:, tc % 2, qi * qw:(qi + 1) * qw],
                       sc2[:, tc, 1:2], r7O, r7A, c7, qw,
                       done_sem=eSTR)
                T.wait_ge(eSTR.h, eSTR.v)
                T.wait_ge(eTR.h, eTR.v)
                for j in range(8):
                    ins = T.matmul(psT[0][:, j * 128:(j + 1) * 128],
                                   hqq[:, qb, j * 128:(j + 1) * 128],
                                   identb[:], start=(j == 0),
                                   stop=(j == 7), is_transpose=True)
                mTR.inc(ins, 1)
                SC.wait_ge(mTR.h, mTR.v)
                eTR.inc(SC.activation(hqT[:, tl, qi * 8:(qi + 1) * 8, :],
                                      psT[0][:], AF.Identity), 1)
                if qi == 3:
                    tr_done[tc] = eTR.v

            for tl in range(4):
                tc = grp * 4 + tl
                stair_q.append(lambda tc=tc, tl=tl: piece_a(tc, tl == 0))
                for qi in range(4):
                    stair_q.append(lambda tc=tc, qi=qi: quarter(tc, qi))

        def down_group(g):
            for tl in range(4):
                tc = g * 4 + tl
                par = tc % 2
                row = slice(tc * 128, (tc + 1) * 128)
                if tc >= 1:
                    SY.wait_ge(eOUT.h, 2 * tc)
                dRL[par].inc(SY.dma_start(out=res_ld[:, 0, :],
                                          in_=s_res[row, :]), 16)
                for os_ in range(2):
                    ps = psDN[os_]
                    T.wait_ge(eOUT.h, dn_gate[os_])
                    T.wait_ge(eTR.h, tr_done[tc])
                    T.wait_ge(dWD.h, dWD.v)
                    for ic in range(NICD):
                        ins = T.matmul(
                            ps,
                            hqT[:, tl, ic, :],
                            wdT[:, ic, os_ * 512:(os_ + 1) * 512],
                            start=(ic == 0), stop=(ic == NICD - 1))
                    mDN.inc(ins, 1)
                    V.wait_ge(mDN.h, mDN.v)
                    V.wait_ge(dRL[par].h, dRL[par].v)
                    if tc >= 1 and os_ == 0:
                        V.wait_ge(dOUT[0].h, dOUT[0].v)
                    eOUT.inc(V.scalar_tensor_tensor(
                        out_sb[:, 0, os_ * 512:(os_ + 1) * 512], ps,
                        sc2[:, tc, 0:1],
                        res_ld[:, 0, os_ * 512:(os_ + 1) * 512],
                        A.mult, A.add), 1)
                    dn_gate[os_] = eOUT.v
                SC.wait_ge(eOUT.h, eOUT.v)
                dOUT[0].inc(SC.dma_start(out=out_hbm[row, :],
                                         in_=out_sb[:, 0, :]), 16)
                drain_stair(1)

        stair_q = []
        tr_done = {}
        u6_group(0)
        dWD.inc(SY.dma_start(out=wdT[:],
                             in_=s_wT["w_d"]
                                 .rearrange("(b p) c -> p b c", b=NICD)), 16)
        mk_stair_pieces(0)
        for grp in range(1, 4):
            u6_group(grp, drain=True)
            drain_stair(20)
            down_group(grp - 1)
            mk_stair_pieces(grp)
        drain_stair(20)
        down_group(3)
        drain_stair(20)

        G.wait_ge(dOUT[0].h, dOUT[0].v)
        G.wait_ge(dOUT[1].h, dOUT[1].v)
        scr_map = {"s_xn": s_xn, "s_xq": s_xq, "s_y": s_y, "s_h": s_h,
                   "s_hq": s_hq, "s_res": s_res, "s_w_g": s_w["w_g"],
                   "s_w_u": s_w["w_u"], "s_w_d": s_w["w_d"],
                   "th3": th3[:], "sc2": sc2[:]}
        ndbg = 0
        for dnm, src in scr_map.items():
            if dnm in dbg:
                dAW.inc(G.dma_start(out=dbg[dnm], in_=src), 16)
                ndbg += 1
        if ndbg:
            G.wait_ge(dAW.h, dAW.v)
        ctxG.close()

    @blk.sync
    def _(_sync):
        try:
            _body(_sync)
        except _Done:
            pass
        for st in (ctxG, ctxF, ctxC, ctxB, ctxQ, ctxU5, ctxPJ, ctxLN):
            st.close()

    ctx.close()
    return nc


def _host_consts():
    ind = np.zeros((NDC, 128, 16), dtype=ml_dtypes.bfloat16)
    indt = np.zeros((16, NDC, 128), dtype=np.float32)
    for dc in range(NDC):
        for p in range(128):
            hcol = 2 * dc + p // 64
            ind[dc, p, hcol] = 1
            indt[hcol, dc, p] = 1.0
    ones = np.ones((128, 128), dtype=np.float32)
    cmask = np.triu(np.ones((128, 128))).astype(ml_dtypes.bfloat16)
    ident = np.eye(128).astype(ml_dtypes.bfloat16)
    consts = np.zeros((128, 8), dtype=np.float32)
    for i, vv in enumerate(CONST_VALS):
        consts[:, i] = vv
    return ind, indt, ones, cmask, ident, consts


def kernel(L0_attn_x, L0_attn_w_q, L0_attn_w_k, L0_attn_w_v, L0_attn_w_o,
           L0_ffn_w_gate, L0_ffn_w_up, L0_ffn_w_down):
    nc = build_nc()
    ind, indt, ones, cmask, ident, consts = _host_consts()
    x = np.ascontiguousarray(np.asarray(L0_attn_x, dtype=np.float32))
    shared = {
        "w_q": np.ascontiguousarray(L0_attn_w_q, dtype=np.float32),
        "w_k": np.ascontiguousarray(L0_attn_w_k, dtype=np.float32),
        "w_v": np.ascontiguousarray(L0_attn_w_v, dtype=np.float32),
        "w_o": np.ascontiguousarray(L0_attn_w_o, dtype=np.float32),
        "w_g": np.ascontiguousarray(L0_ffn_w_gate, dtype=np.float32),
        "w_u": np.ascontiguousarray(L0_ffn_w_up, dtype=np.float32),
        "w_d": np.ascontiguousarray(L0_ffn_w_down, dtype=np.float32),
        "ind": ind, "indt": indt, "ones": ones, "cmask": cmask,
        "ident": ident,
        "consts": consts,
    }
    in_maps = [{"x": np.ascontiguousarray(x[i]), **shared} for i in range(8)]
    res = run_bass_kernel_spmd(nc, in_maps, core_ids=list(range(8)))
    return np.stack([res.results[i]["out"] for i in range(8)], axis=0)


# version 09
# speedup vs baseline: 1.0091x; 1.0091x over previous
import numpy as np
import ml_dtypes
import concourse.bass as bass
import concourse.mybir as mybir
from concourse.bass_utils import run_bass_kernel_spmd
from contextlib import ExitStack

F32 = mybir.dt.float32
BF16 = mybir.dt.bfloat16
A = mybir.AluOpType
AF = mybir.ActivationFunctionType
AX = mybir.AxisListType

S, D, H, DK, INNER = 2048, 1024, 16, 64, 4096
NTC = S // 128
NDC = D // 128
NIC = INNER // 512
NICD = INNER // 128
EPS_LN = 1e-5
EPS_RMS = 1.1920929e-07
SOFTCAP = 50.0

CONST_VALS = [EPS_LN, EPS_RMS, -8.0, -0.5e-5, 0.5e-5, -1e-5,
              128.5, -1.0, -2.0, -4.0, -16.0, -32.0, -64.0,
              -128.0]


class Sem:
    def __init__(self, ctx, nc, name):
        self.h = ctx.enter_context(nc.semaphore(name))
        self.v = 0

    def inc(self, instr, n=1):
        instr.then_inc(self.h, n)
        self.v += n
        return instr


def _stair(nc, out_bf, src_f32, scale_ap, rO, rA, cb, fd, done_sem=None):
    v = nc.vector
    v.tensor_scalar(rO[:, 0:fd], src_f32, scale_ap, 128.5, A.mult, A.add)
    cur = rO
    for k in range(7, -1, -1):
        step = float(1 << k)
        v.tensor_scalar(cb[:, 0:fd], cur[:, 0:fd], step, step, A.is_ge, A.mult)
        nxt = rA if cur is rO else cur
        v.tensor_sub(nxt[:, 0:fd], cur[:, 0:fd], cb[:, 0:fd])
        cur = nxt
    ins = v.scalar_tensor_tensor(out_bf, rO[:, 0:fd], -128.0, cur[:, 0:fd],
                                 A.add, A.subtract)
    if done_sem is not None:
        done_sem.inc(ins, 1)


def build_nc(debug=(), stop_after=None, proj_mode=0):
    nc = bass.Bass("TRN2", target_bir_lowering=False, debug=False,
                   detect_race_conditions=False)

    x_in = nc.dram_tensor("x", [S, D], F32, kind="ExternalInput").ap()
    w_shapes = (("w_q", [D, D]), ("w_k", [D, D]), ("w_v", [D, D]),
                ("w_o", [D, D]), ("w_g", [INNER, D]), ("w_u", [INNER, D]),
                ("w_d", [D, INNER]))
    w_in = {nm: nc.dram_tensor(nm, sh, F32, kind="ExternalInput").ap()
            for nm, sh in w_shapes}
    ind_in = nc.dram_tensor("ind", [NDC, 128, 16], BF16, kind="ExternalInput").ap()
    indt_in = nc.dram_tensor("indt", [16, NDC, 128], F32, kind="ExternalInput").ap()
    ones_in = nc.dram_tensor("ones", [128, 128], F32, kind="ExternalInput").ap()
    cmask_in = nc.dram_tensor("cmask", [128, 128], BF16, kind="ExternalInput").ap()
    ident_in = nc.dram_tensor("ident", [128, 128], BF16, kind="ExternalInput").ap()
    consts_in = nc.dram_tensor("consts", [128, 16], F32, kind="ExternalInput").ap()
    out_hbm = nc.dram_tensor("out", [S, D], F32, kind="ExternalOutput").ap()
    dbg = {}
    for name, shape, dt in debug:
        dbg[name] = nc.dram_tensor("dbg_" + name, shape, dt,
                                   kind="ExternalOutput").ap()

    s_w = {nm: nc.dram_tensor("s_" + nm, sh, BF16, kind="Internal").ap()
           for nm, sh in w_shapes}
    s_wT = {"w_g": nc.dram_tensor("s_wT_g", [D, INNER], BF16, kind="Internal").ap(),
            "w_u": nc.dram_tensor("s_wT_u", [D, INNER], BF16, kind="Internal").ap(),
            "w_d": nc.dram_tensor("s_wT_d", [INNER, D], BF16, kind="Internal").ap()}
    s_xn = nc.dram_tensor("s_xn", [S, D], BF16, kind="Internal").ap()
    s_xq = nc.dram_tensor("s_xq", [S, D], BF16, kind="Internal").ap()
    s_y = nc.dram_tensor("s_y", [S, D], BF16, kind="Internal").ap()
    s_h = nc.dram_tensor("s_h", [S, INNER], F32, kind="Internal").ap()
    s_hq = nc.dram_tensor("s_hq", [S, INNER], BF16, kind="Internal").ap()
    s_res = nc.dram_tensor("s_res", [S, D], F32, kind="Internal").ap()

    ctx = ExitStack()
    ctxLN = ExitStack()
    ctxPJ = ExitStack()
    ctxU5 = ExitStack()
    ctxQ = ExitStack()
    ctxB = ExitStack()
    ctxC = ExitStack()
    ctxF = ExitStack()
    ctxG = ExitStack()
    sb = lambda st, name, shape, dt: st.enter_context(nc.sbuf_tensor(name, shape, dt))
    sbr = lambda st, name, shape, dt: st.enter_context(nc.sbuf_tensor(name, shape, dt, side="right"))

    cst = sb(ctx, "cst", [128, 16], F32)
    ind = sb(ctx, "indb", [128, NDC, 16], BF16)
    ones = sb(ctx, "onesb", [128, 128], F32)
    cmask = sb(ctx, "cmaskb", [128, 128], BF16)
    identb = sb(ctx, "identb", [128, 128], BF16)
    stats = sb(ctx, "stats", [128, NTC, 10], F32)
    th3 = sb(ctx, "th3", [128, 8], F32)
    rmax = sb(ctx, "rmax", [128, 2, NTC], F32)
    sc2 = sb(ctx, "sc2", [128, NTC, 4], F32)

    for i, vv in enumerate(CONST_VALS):
        nc.const_aps.register(F32, vv, cst[:, i:i + 1])

    wstg = sb(ctxU5, "wstg", [128, 2, D], F32)
    accA = sb(ctxU5, "accA", [128, 3, 32], F32)
    sgA = sb(ctxU5, "sgA", [128, 2, D], BF16)
    sgB = sb(ctxU5, "sgB", [128, 2, D], BF16)
    w3o = sb(ctxU5, "w3o", [128, 2, D], BF16)
    wte = sb(ctxU5, "wte", [128, 2, D], BF16)

    xld = sb(ctxLN, "xld", [128, 2, D], F32)
    xc2 = sb(ctxLN, "xc2", [128, 2, D], F32)
    xsq = sb(ctxLN, "xsq", [128, D], F32)
    stgn = sb(ctxLN, "stgn", [128, 2, D], BF16)
    stgq = sb(ctxLN, "stgq", [128, 2, D], BF16)
    rO = sb(ctxLN, "rO", [128, D], F32)
    rA = sb(ctxLN, "rA", [128, D], F32)
    cbf = sb(ctxLN, "cbf", [128, D], F32)

    xn_fm = sbr(ctxPJ, "xn_fm", [128, NDC, S], BF16)
    indt = sbr(ctxPJ, "indtb", [16, NDC, 128], F32)
    wT = sbr(ctxPJ, "wT", [128, NDC, D], BF16)
    qsq = sbr(ctxPJ, "qsq", [128, 2, 512], BF16)
    a_q = sbr(ctxPJ, "a_q", [16, S], F32)
    a_ln = sbr(ctxPJ, "a_ln", [16, 512], F32)

    psU1 = ctx.enter_context(nc.psum_tensor("psU1", [128, 512], F32))
    psU2 = ctx.enter_context(nc.psum_tensor("psU2", [128, 512], F32))
    psSCa = ctx.enter_context(nc.psum_tensor("psSCa", [128, 1024], F32))
    psSCb = ctx.enter_context(nc.psum_tensor("psSCb", [128, 1024], F32))
    psY = ctx.enter_context(nc.psum_tensor("psY", [128, 512], F32))
    ctxPSS = ExitStack()
    psSS = ctxPSS.enter_context(nc.psum_tensor("psSS", [16, 512], F32))
    psT = [None]

    all_sems = []

    def sm(name):
        x = Sem(ctx, nc, name)
        all_sems.append(x)
        return x
    dX = [sm("dXa"), sm("dXb")]
    dXN = [sm("dXNa"), sm("dXNb")]
    dXNL, dXQL = sm("dXNL"), sm("dXQL")
    dAW, dWT = sm("dAW"), sm("dWT")
    sLN1, sLN2, sLN3, sLN4, sLN5 = sm("sLN1"), sm("sLN2"), sm("sLN3"), sm("sLN4"), sm("sLN5")
    sA1, sA2, sA3, sA4 = sm("sA1"), sm("sA2"), sm("sA3"), sm("sA4")
    mPJ, sSQ, sCP, mSS, sA, mBC, eRMS = sm("mPJ"), sm("sSQ"), sm("sCP"), sm("mSS"), sm("sA"), sm("mBC"), sm("eRMS")
    dF1 = [sm("dF1a"), sm("dF1b")]
    dF2 = [sm("dF2a"), sm("dF2b")]
    dW3 = [sm("dW3a"), sm("dW3b")]
    sAB, mMean, eTH, sSG, eW3 = sm("sAB"), sm("mMean"), sm("eTH"), sm("sSG"), sm("eW3")
    mQK, sTNH, sEXP, eMSK, mPV, eREC, sYN = sm("mQK"), sm("sTNH"), sm("sEXP"), sm("eMSK"), sm("mPV"), sm("eREC"), sm("sYN")
    dGU = [sm("dGUa"), sm("dGUb")]
    dH = [sm("dHa"), sm("dHb")]
    mGU, sSIL, eH = sm("mGU"), sm("sSIL"), sm("eH")
    eH2, eAS, eAS2, eH3, eW3b, eRM = sm("eH2"), sm("eAS"), sm("eAS2"), sm("eH3"), sm("eW3b"), sm("eRM")
    eCL, eRC = sm("eCL"), sm("eRC")
    mTW, eWT, mTR, eTR = sm("mTW"), sm("eWT"), sm("mTR"), sm("eTR")
    sSB, sVB = sm("sSB"), sm("sVB")
    dY, dYL, dWO, mO, eRS = sm("dY"), sm("dYL"), sm("dWO"), sm("mO"), sm("eRS")
    dRS = [sm("dRSa"), sm("dRSb")]
    dXR = [sm("dXRa"), sm("dXRb")]
    dHL, eSTR, dHQL, dWD, mDN, eOUT = sm("dHL"), sm("eSTR"), sm("dHQL"), sm("dWD"), sm("mDN"), sm("eOUT")
    dHQq = [sm("dHQ0"), sm("dHQ1"), sm("dHQ2"), sm("dHQ3")]
    dRL = [sm("dRLa"), sm("dRLb")]
    dOUT = [sm("dOUTa"), sm("dOUTb")]

    blk = ctx.enter_context(nc.Block())

    class _Done(Exception):
        pass

    def _body(_sync):
        T, V, SC, G, SY = nc.tensor, nc.vector, nc.scalar, nc.gpsimd, nc.sync

        def probe_stop(tag):
            if stop_after == tag:
                for sx in all_sems:
                    if sx.v:
                        nc.gpsimd.wait_ge(sx.h, sx.v)
                if "stats" in dbg:
                    dAW.inc(nc.gpsimd.dma_start(out=dbg["stats"], in_=stats[:]), 16)
                if "s_xn" in dbg:
                    dAW.inc(nc.gpsimd.dma_start(out=dbg["s_xn"], in_=s_xn), 16)
                if "s_xq" in dbg:
                    dAW.inc(nc.gpsimd.dma_start(out=dbg["s_xq"], in_=s_xq), 16)
                nc.gpsimd.wait_ge(dAW.h, dAW.v)
                raise _Done

        dAW.inc(G.dma_start(out=cst[:], in_=consts_in), 16)
        for dc in range(NDC):
            dAW.inc(G.dma_start(out=ind[:, dc, :], in_=ind_in[dc]), 16)
        dAW.inc(G.dma_start(out=ones[:], in_=ones_in), 16)
        dAW.inc(G.dma_start(out=cmask[:], in_=cmask_in), 16)
        dAW.inc(G.dma_start(out=identb[:], in_=ident_in), 16)
        for nm in ("w_q", "w_k", "w_v", "w_o"):
            dAW.inc(G.dma_start(out=s_w[nm], in_=w_in[nm]), 16)

        WMETA = {"g": ("w_g", 0), "u": ("w_u", 1), "d": ("w_d", 2)}

        def u5_chunk_ap(nm, chi):
            if nm == "w_d":
                r, c = chi // 4, chi % 4
                return (slice(r * 128, (r + 1) * 128),
                        slice(c * 1024, (c + 1) * 1024))
            return (slice(chi * 128, (chi + 1) * 128), slice(0, D))

        def u5_s1(m, lo, hi):
            nm, widx = WMETA[m]
            for chi in range(lo, hi):
                par = chi % 2
                SY.wait_ge(sAB.h, max(0, sAB.v - 1))
                SY.wait_ge(sSG.h, sSG.v)
                dF1[par].inc(SY.dma_start(out=wstg[:, par, :],
                                     in_=w_in[nm][u5_chunk_ap(nm, chi)]), 16)
                SC.wait_ge(dF1[par].h, dF1[par].v)
                sAB.inc(SC.activation(sgA[:, par, :], wstg[:, par, :], AF.Abs,
                                      accum_out=accA[:, widx, chi:chi + 1]), 1)

        def u5_fin(m):
            _, col = WMETA[m]
            T.wait_ge(sAB.h, sAB.v)
            T.wait_ge(sCP.h, sCP.v)
            T.wait_ge(eW3b.h, eW3b.v)
            mMean.inc(T.matmul(psU1[:, 0:32], ones[:], accA[:, col, :],
                               start=True, stop=True), 1)
            V.wait_ge(mMean.h, mMean.v)
            eW3b.inc(V.tensor_reduce(th3[:, 4:5], psU1[:, 0:32], axis=AX.X,
                                     op=A.add), 1)
            SC.wait_ge(eW3b.h, eW3b.v)
            SC.activation(th3[:, 5:6], th3[:, 4:5], AF.Relu,
                          scale=0.5 / (INNER * D), bias=-0.5e-5)
            eTH.inc(SC.activation(th3[:, col:col + 1], th3[:, 5:6], AF.Identity,
                                  bias=0.5e-5), 1)

        def u5T_view(nm, chi):
            if nm == "w_d":
                r, c = chi // 4, chi % 4
                reg = s_wT[nm][c * 1024:(c + 1) * 1024, r * 128:(r + 1) * 128]
            else:
                reg = s_wT[nm][:, chi * 128:(chi + 1) * 128]
            return reg.rearrange("(b p) w -> p b w", b=8)

        w3_ready = {}

        def u5_s2a(m, lo, hi):
            nm, col = WMETA[m]
            for chi in range(lo, hi):
                par = chi % 2
                SY.wait_ge(sSG.h, max(0, sSG.v - 1))
                SY.wait_ge(sAB.h, sAB.v)
                dF2[par].inc(SY.dma_start(out=wstg[:, par, :],
                                     in_=w_in[nm][u5_chunk_ap(nm, chi)]), 16)
                V.wait_ge(dF2[par].h, dF2[par].v)
                V.wait_ge(eTH.h, eTH.v)
                V.wait_ge(eW3.h, max(0, eW3.v - 1))
                V.tensor_scalar(sgA[:, par, :], wstg[:, par, :],
                                th3[:, col:col + 1], 2.0, A.is_ge, A.mult)
                sSG.inc(V.tensor_scalar(sgB[:, par, :], wstg[:, par, :],
                                        th3[:, 5 + col:6 + col], 2.0,
                                        A.is_le, A.mult), 1)
                V.wait_ge(dW3[par].h, dW3[par].v)
                eW3.inc(V.tensor_sub(w3o[:, par, :], sgA[:, par, :],
                                     sgB[:, par, :]), 1)
                w3_ready[(m, chi)] = eW3.v

        def u5_s2b(m, lo, hi):
            nm, col = WMETA[m]
            for chi in range(lo, hi):
                par = chi % 2
                T.wait_ge(eW3.h, w3_ready[(m, chi)])
                T.wait_ge(eWT.h, eWT.v)
                for j in range(NDC):
                    ins = T.matmul(psT[0][:, j * 128:(j + 1) * 128],
                                   w3o[:, par, j * 128:(j + 1) * 128],
                                   identb[:], start=(j == 0), stop=(j == NDC - 1),
                                   is_transpose=True)
                mTW.inc(ins, 1)
                V.wait_ge(mTW.h, mTW.v)
                eWT.inc(V.tensor_copy(wte[:, par, :], psT[0][:]), 1)
                G.wait_ge(eWT.h, eWT.v)
                dW3[par].inc(G.dma_start(
                    out=u5T_view(nm, chi),
                    in_=wte[:, par, :].rearrange("p (b w) -> p b w", b=8)), 16)

        u5_plan = []
        for m in ("g", "u", "d"):
            for j in range(16):
                u5_plan.append(("s1", m, j * 2, (j + 1) * 2))
        s2a = [("s2a", m, j, j + 1) for m in ("g", "u", "d") for j in range(32)]
        s2b = [("s2b", m, j, j + 1) for m in ("g", "u", "d") for j in range(32)]
        for i in range(len(s2a)):
            u5_plan.append(s2a[i])
            if i >= 1:
                u5_plan.append(s2b[i - 1])
        u5_plan.append(s2b[-1])
        u5_pos = [0]

        def u5_pieces(n):
            for _ in range(n):
                if u5_pos[0] >= len(u5_plan):
                    return
                e = u5_plan[u5_pos[0]]
                u5_pos[0] += 1
                if e[0] == "s1":
                    u5_s1(e[1], e[2], e[3])
                elif e[0] == "fin":
                    u5_fin(e[1])
                elif e[0] == "s2a":
                    u5_s2a(e[1], e[2], e[3])
                else:
                    u5_s2b(e[1], e[2], e[3])


        for tc in range(NTC):
            par = tc % 2
            row = slice(tc * 128, (tc + 1) * 128)
            if tc >= 2:
                SY.wait_ge(sLN5.h, tc - 1)
            dX[par].inc(SY.dma_start(out=xld[:, par, :], in_=x_in[row, :]), 16)
            V.wait_ge(dX[par].h, dX[par].v)
            sLN1.inc(V.tensor_reduce(stats[:, tc, 0:1], xld[:, par, :],
                                     axis=AX.X, op=A.add, negate=True), 1)
            SC.wait_ge(sLN1.h, sLN1.v)
            sA1.inc(SC.activation(stats[:, tc, 1:2], stats[:, tc, 0:1],
                                  AF.Identity, scale=1.0 / D), 1)
            V.wait_ge(sA1.h, sA1.v)
            if tc >= 2:
                V.wait_ge(sLN5.h, tc - 1)
            sA2.inc(V.tensor_scalar(xc2[:, par, :], xld[:, par, :],
                                    stats[:, tc, 1:2], None, A.add), 1)
            SC.wait_ge(sA2.h, sA2.v)
            sLN2.inc(SC.activation(xsq[:], xc2[:, par, :], AF.Square), 1)
            V.wait_ge(sLN2.h, sLN2.v)
            V.tensor_reduce(stats[:, tc, 2:3], xsq[:], axis=AX.X, op=A.add)
            sLN3.inc(V.tensor_reduce(stats[:, tc, 3:4], xc2[:, par, :],
                                     axis=AX.X, op=A.max,
                                     apply_absolute_value=True), 1)
            SC.wait_ge(sLN3.h, sLN3.v)
            SC.activation(stats[:, tc, 4:5], stats[:, tc, 2:3], AF.Ln,
                          scale=1.0 / D, bias=EPS_LN)
            sLN4.inc(SC.activation(stats[:, tc, 5:6], stats[:, tc, 4:5],
                                   AF.Exp, scale=-0.5), 1)
            V.wait_ge(sLN4.h, sLN4.v)
            sA3.inc(V.tensor_scalar(stats[:, tc, 6:7], stats[:, tc, 3:4],
                                    stats[:, tc, 5:6], 1e-5, A.mult, A.max), 1)
            SC.wait_ge(sA3.h, sA3.v)
            SC.activation(stats[:, tc, 4:5], stats[:, tc, 6:7], AF.Ln)
            sA4.inc(SC.activation(stats[:, tc, 7:8], stats[:, tc, 4:5],
                                  AF.Exp, scale=-1.0), 1)
            V.wait_ge(sA4.h, sA4.v)
            V.tensor_scalar(stats[:, tc, 8:9], stats[:, tc, 7:8],
                            stats[:, tc, 5:6], 127.0, A.mult, A.mult)
            V.tensor_scalar(stats[:, tc, 9:10], stats[:, tc, 6:7], 1.0 / 127.0,
                            None, A.mult)
            if tc >= 2:
                V.wait_ge(dXN[par].h, dXN[par].v)
            V.tensor_scalar(stgn[:, par, :], xc2[:, par, :], stats[:, tc, 5:6],
                            None, A.mult)
            _stair(nc, stgq[:, par, :], xc2[:, par, :], stats[:, tc, 8:9],
                   rO, rA, cbf, D, done_sem=sLN5)
            G.wait_ge(sLN5.h, sLN5.v)
            dXN[par].inc(G.dma_start(out=s_xn[row, :], in_=stgn[:, par, :]), 16)
            dXN[par].inc(G.dma_start(out=s_xq[row, :], in_=stgq[:, par, :]), 16)
            u5_pieces(3)

        for m in ("g", "u", "d"):
            u5_fin(m)
        for i in range(3):
            ins = SC.activation(th3[:, 5 + i:6 + i], th3[:, i:i + 1],
                                AF.Identity, scale=-1.0)
        eTH.inc(ins, 1)
        T.wait_ge(eW3b.h, eW3b.v)

        probe_stop("u0")
        dAW.inc(G.dma_start(out=indt[0:16, :, :], in_=indt_in), 16)
        SY.wait_ge(dXN[0].h, dXN[0].v)
        SY.wait_ge(dXN[1].h, dXN[1].v)
        for dc in range(NDC):
            dXNL.inc(SY.dma_start(out=xn_fm[:, dc, :],
                                  in_=s_xn[:, dc * 128:(dc + 1) * 128],
                                  transpose=True), 16)
        ctxLN.close()

        q_fm = sb(ctxQ, "q_fm", [128, NDC, S], BF16)
        k_fm = sb(ctxQ, "k_fm", [128, NDC, S], BF16)
        v_ext = sb(ctxQ, "v_ext", [128, NTC, 16, 65], BF16)
        V.memset(v_ext[:], 1.0)

        evac_gate = {id(psU1): (0, 0), id(psU2): (0, 0)}

        def proj_qk(wscr, dst_fm):
            SY.wait_ge(dAW.h, dAW.v)
            SY.wait_ge(mPJ.h, mPJ.v)
            for dc in range(NDC):
                dWT.inc(SY.dma_start(out=wT[:, dc, :],
                                     in_=wscr[:, dc * 128:(dc + 1) * 128],
                                     transpose=True), 16)
            probe_stop("uqw")
            for ts in range(4):
                tsl = slice(ts * 512, (ts + 1) * 512)
                for oc in range(NDC):
                    ps = psU1 if oc % 2 == 0 else psU2
                    g1, g2 = evac_gate[id(ps)]
                    T.wait_ge(sSQ.h, g1)
                    T.wait_ge(sCP.h, g2)
                    T.wait_ge(dWT.h, dWT.v)
                    T.wait_ge(dXNL.h, dXNL.v)
                    for dc in range(NDC):
                        ins = T.matmul(ps[:], wT[:, dc, oc * 128:(oc + 1) * 128],
                                       xn_fm[:, dc, tsl], start=(dc == 0),
                                       stop=(dc == NDC - 1))
                    mPJ.inc(ins, 1)
                    par2 = oc % 2
                    SC.wait_ge(mPJ.h, mPJ.v)
                    sSQ.inc(SC.activation(qsq[:, par2, :], ps[:], AF.Square), 1)
                    V.wait_ge(sSQ.h, sSQ.v)
                    sCP.inc(V.tensor_copy(dst_fm[:, oc, tsl], ps[:]), 1)
                    evac_gate[id(ps)] = (sSQ.v, sCP.v)
                    probe_stop("uq0")
                    if proj_mode >= 2:
                        continue
                    T.wait_ge(sSQ.h, sSQ.v)
                    if oc == 0:
                        T.wait_ge(sA.h, sA.v)
                    ins2 = T.matmul(psSS[:], ind[:, oc, :], qsq[:, par2, :],
                                    start=(oc == 0), stop=(oc == NDC - 1))
                if proj_mode >= 2:
                    continue
                mSS.inc(ins2, 1)
                SC.wait_ge(mSS.h, mSS.v)
                SC.activation(a_ln[:], psSS[:], AF.Ln, scale=1.0 / DK,
                              bias=EPS_RMS)
                sA.inc(SC.activation(a_q[:, tsl], a_ln[:], AF.Exp, scale=-0.5), 1)
                if proj_mode >= 1:
                    continue
                for oc in range(NDC):
                    T.wait_ge(sA.h, sA.v)
                    T.wait_ge(eRMS.h, eRMS.v)
                    mBC.inc(T.matmul(psY[:], indt[0:16, oc, :], a_q[0:16, tsl],
                                     start=True, stop=True), 1)
                    V.wait_ge(mBC.h, mBC.v)
                    eRMS.inc(V.tensor_mul(dst_fm[:, oc, tsl], dst_fm[:, oc, tsl],
                                          psY[:]), 1)

        proj_qk(s_w["w_q"], q_fm)
        probe_stop("uq")
        proj_qk(s_w["w_k"], k_fm)

        SY.wait_ge(mPJ.h, mPJ.v)
        for dc in range(NDC):
            dWT.inc(SY.dma_start(out=wT[:, dc, :],
                                 in_=s_w["w_v"][:, dc * 128:(dc + 1) * 128],
                                 transpose=True), 16)
        for tc in range(NTC):
            for vs in range(2):
                ps = psU1 if (tc * 2 + vs) % 2 == 0 else psU2
                g1, g2 = evac_gate[id(ps)]
                T.wait_ge(sCP.h, g2)
                T.wait_ge(dWT.h, dWT.v)
                for dc in range(NDC):
                    ins = T.matmul(ps[:], xn_fm[:, dc, tc * 128:(tc + 1) * 128],
                                   wT[:, dc, vs * 512:(vs + 1) * 512],
                                   start=(dc == 0), stop=(dc == NDC - 1))
                mPJ.inc(ins, 1)
                V.wait_ge(mPJ.h, mPJ.v)
                sCP.inc(V.tensor_copy(
                    v_ext[:, tc, vs * 8:(vs + 1) * 8, 0:64],
                    ps[:].rearrange("p (h f) -> p h f", h=8)), 1)
                evac_gate[id(ps)] = (sSQ.v, sCP.v)

        ctxPJ.close()
        ctxPSS.close()
        psT[0] = ctx.enter_context(nc.psum_tensor("psT", [128, 1024], BF16))
        probe_stop("u3")
        y_tm = sbr(ctxB, "y_tm", [128, NTC, D], BF16)
        t1b = sbr(ctxB, "t1b", [128, 2, 1024], F32)
        p_sb = sbr(ctxB, "p_sb", [128, 2, 1024], BF16)
        den = sbr(ctxB, "den", [128, 4], F32)

        T.wait_ge(eRMS.h, eRMS.v)
        T.wait_ge(sCP.h, sCP.v)
        for dnm, dbuf in (("q_fm", q_fm), ("k_fm", k_fm), ("v_ext", v_ext)):
            if dnm in dbg:
                G.wait_ge(eRMS.h, eRMS.v)
                G.wait_ge(sCP.h, sCP.v)
                dAW.inc(G.dma_start(out=dbg[dnm], in_=dbuf[:]), 16)

        pair_idx = [0]
        sc_gate = [0, 0]
        p_gate = [0, 0]

        for h in range(H):
            oc = h // 2
            pr = (h % 2) * 64
            for qt in range(4):
                qsl = slice(qt * 512, (qt + 1) * 512)
                T.wait_ge(sYN.h, h * 4 + qt)
                nr = 4 * qt + 4
                for j in range(nr // 2):
                    r0, r1 = 2 * j, 2 * j + 1
                    par = pair_idx[0] % 2
                    ps = psSCa if par == 0 else psSCb
                    T.wait_ge(sTNH.h, sc_gate[par])
                    for rr, half in ((r0, 0), (r1, 1)):
                        ins = T.matmul(ps[:, half * 512:(half + 1) * 512],
                                       k_fm[pr:pr + 64, oc, rr * 128:(rr + 1) * 128],
                                       q_fm[pr:pr + 64, oc, qsl],
                                       start=True, stop=True)
                    mQK.inc(ins, 1)
                    SC.wait_ge(mQK.h, mQK.v)
                    SC.wait_ge(mPV.h, p_gate[par])
                    sTNH.inc(SC.activation(t1b[:, par, :], ps[:], AF.Tanh,
                                           scale=1.0 / (8.0 * SOFTCAP)), 1)
                    sc_gate[par] = sTNH.v
                    sEXP.inc(SC.activation(p_sb[:, par, :], t1b[:, par, :],
                                           AF.Exp, scale=SOFTCAP, bias=-8.0), 1)
                    nmask = 0
                    for rr, half in ((r0, 0), (r1, 1)):
                        if 4 * qt <= rr:
                            V.wait_ge(sEXP.h, sEXP.v)
                            sl = half * 512 + (rr - 4 * qt) * 128
                            eMSK.inc(V.tensor_mul(p_sb[:, par, sl:sl + 128],
                                                  p_sb[:, par, sl:sl + 128],
                                                  cmask[:]), 1)
                            nmask += 1
                    T.wait_ge(sEXP.h, sEXP.v)
                    if nmask:
                        T.wait_ge(eMSK.h, eMSK.v)
                    for rr, half in ((r0, 0), (r1, 1)):
                        for cl in range(4):
                            c = 4 * qt + cl
                            if c < rr:
                                continue
                            ins = T.matmul(psY[:, cl * 128:cl * 128 + 65],
                                           p_sb[:, par, half * 512 + cl * 128:
                                                half * 512 + (cl + 1) * 128],
                                           v_ext[:, rr, h, :],
                                           start=(rr == 0 and cl == 0),
                                           stop=(rr == 4 * qt + 3 and cl == 3))
                    mPV.inc(ins, 1)
                    p_gate[par] = mPV.v
                    pair_idx[0] += 1
                V.wait_ge(mPV.h, mPV.v)
                for cl in range(4):
                    ins = V.reciprocal(den[:, cl:cl + 1],
                                       psY[:, cl * 128 + 64:cl * 128 + 65])
                eREC.inc(ins, 1)
                SC.wait_ge(eREC.h, eREC.v)
                for cl in range(4):
                    ins = SC.activation(y_tm[:, 4 * qt + cl, h * 64:(h + 1) * 64],
                                        psY[:, cl * 128:cl * 128 + 64],
                                        AF.Identity, scale=den[:, cl:cl + 1])
                sYN.inc(ins, 1)
                u5_pieces(3)
        u5_pieces(100)

        probe_stop("u4")
        ctxQ.close()
        ctxU5.close()

        y_fm = sb(ctxC, "y_fm", [128, NDC, S], BF16)
        woT = sb(ctxC, "woT", [128, NDC, D], BF16)
        x_re = sb(ctxC, "x_re", [128, 2, D], F32)
        res_sb = sb(ctxC, "res_sb", [128, 2, D], F32)

        G.wait_ge(sYN.h, sYN.v)
        if "y_tm" in dbg:
            dAW.inc(G.dma_start(out=dbg["y_tm"], in_=y_tm[:]), 16)
        for tc in range(NTC):
            dY.inc(G.dma_start(out=s_y[tc * 128:(tc + 1) * 128, :],
                               in_=y_tm[:, tc, :]), 16)
        SY.wait_ge(dY.h, dY.v)
        SY.wait_ge(dW3[0].h, dW3[0].v)
        SY.wait_ge(dW3[1].h, dW3[1].v)
        for dc in range(NDC):
            dYL.inc(SY.dma_start(out=y_fm[:, dc, :],
                                 in_=s_y[:, dc * 128:(dc + 1) * 128],
                                 transpose=True), 16)
            dWO.inc(SY.dma_start(out=woT[:, dc, :],
                                 in_=s_w["w_o"][:, dc * 128:(dc + 1) * 128],
                                 transpose=True), 16)
        T.wait_ge(eTH.h, eTH.v)
        res_gate = {id(psU1): 0, id(psU2): 0}
        for tc in range(NTC):
            par = tc % 2
            row = slice(tc * 128, (tc + 1) * 128)
            if tc >= 2:
                SY.wait_ge(eRS.h, 2 * (tc - 1))
            dXR[par].inc(SY.dma_start(out=x_re[:, par, :], in_=x_in[row, :]), 16)
            for os_ in range(2):
                ps = psU1 if os_ == 0 else psU2
                T.wait_ge(eRS.h, res_gate[id(ps)])
                T.wait_ge(dYL.h, dYL.v)
                T.wait_ge(dWO.h, dWO.v)
                for dc in range(NDC):
                    ins = T.matmul(ps[:], y_fm[:, dc, tc * 128:(tc + 1) * 128],
                                   woT[:, dc, os_ * 512:(os_ + 1) * 512],
                                   start=(dc == 0), stop=(dc == NDC - 1))
                mO.inc(ins, 1)
                V.wait_ge(mO.h, mO.v)
                V.wait_ge(dXR[par].h, dXR[par].v)
                if tc >= 2 and os_ == 0:
                    V.wait_ge(dRS[par].h, dRS[par].v)
                eRS.inc(V.tensor_add(res_sb[:, par, os_ * 512:(os_ + 1) * 512],
                                     ps[:],
                                     x_re[:, par, os_ * 512:(os_ + 1) * 512]), 1)
                res_gate[id(ps)] = eRS.v
            G.wait_ge(eRS.h, eRS.v)
            dRS[par].inc(G.dma_start(out=s_res[row, :], in_=res_sb[:, par, :]), 16)
        ctxB.close()
        ctxC.close()

        probe_stop("u8")
        xqg = sb(ctxF, "xqg", [128, 2, NDC, 512], BF16)
        gusl = sb(ctxF, "gusl", [128, 2, 2, NDC, 512], BF16)
        gate_sb = sb(ctxF, "gate_sb", [128, 512], F32)
        h_sb = sb(ctxF, "h_sb", [128, 2, 512], F32)
        hload = sb(ctxF, "hload", [128, 2, INNER], F32)
        r7O = sb(ctxF, "r7O", [128, INNER // 4], F32)
        r7A = sb(ctxF, "r7A", [128, INNER // 4], F32)
        c7 = psSCb
        hqq = sb(ctxF, "hqq", [128, 2, 1024], BF16)
        wdT = sb(ctxF, "wdT", [128, NICD, D], BF16)
        hqT = sb(ctxF, "hqT", [128, 4, NICD, 128], BF16)
        res_ld = sb(ctxF, "res_ld", [128, 1, D], F32)
        out_sb = sb(ctxF, "out_sb", [128, 1, D], F32)

        V.wait_ge(eTH.h, eTH.v)
        for tc in range(NTC):
            V.tensor_scalar(sc2[:, tc, 2:3], stats[:, tc, 9:10],
                            th3[:, 0:1], None, A.mult)
            V.tensor_scalar(sc2[:, tc, 3:4], stats[:, tc, 9:10],
                            th3[:, 1:2], None, A.mult)
        SY.wait_ge(sYN.h, 4 * H)
        SY.wait_ge(dY.h, dY.v)
        SY.wait_ge(dRS[0].h, dRS[0].v)
        SY.wait_ge(dRS[1].h, dRS[1].v)
        SY.wait_ge(dW3[0].h, dW3[0].v)
        SY.wait_ge(dW3[1].h, dW3[1].v)
        T.wait_ge(eRS.h, eRS.v)
        T.wait_ge(sTNH.h, sTNH.v)
        gu_gate = [0, 0]
        xq_gate = [0, 0]
        psg = [0, 0]
        dn_gate = [0, 0]
        psDN = (psSCa[:, 0:512], psSCa[:, 512:1024])
        dXQg = [dXQL, dYL]

        def drain_stair(n=1):
            for _ in range(n):
                if stair_q:
                    stair_q.pop(0)()

        def u6_group(grp, drain=False):
            xbuf = grp % 2
            SY.wait_ge(mGU.h, xq_gate[xbuf])
            for dc in range(NDC):
                dXQg[xbuf].inc(SY.dma_start(
                    out=xqg[:, xbuf, dc, :],
                    in_=s_xq[grp * 512:(grp + 1) * 512,
                             dc * 128:(dc + 1) * 128],
                    transpose=True), 16)
            for ic in range(NIC):
                buf = (grp * NIC + ic) % 2
                SY.wait_ge(mGU.h, gu_gate[buf])
                for w, nm in ((0, "w_g"), (1, "w_u")):
                    dGU[buf].inc(SY.dma_start(
                        out=gusl[:, w, buf, :, :],
                        in_=s_wT[nm][:, ic * 512:(ic + 1) * 512]
                            .rearrange("(b p) c -> p b c", b=NDC)), 16)
                for tl in range(4):
                    tc = grp * 4 + tl
                    T.wait_ge(dGU[buf].h, dGU[buf].v)
                    T.wait_ge(dXQg[xbuf].h, dXQg[xbuf].v)
                    T.wait_ge(sSIL.h, psg[0])
                    for dc in range(NDC):
                        T.matmul(psU1[:],
                                 xqg[:, xbuf, dc, tl * 128:(tl + 1) * 128],
                                 gusl[:, 0, buf, dc, :], start=(dc == 0),
                                 stop=(dc == NDC - 1))
                    T.wait_ge(eH.h, psg[1])
                    for dc in range(NDC):
                        ins = T.matmul(psU2[:],
                                       xqg[:, xbuf, dc, tl * 128:(tl + 1) * 128],
                                       gusl[:, 1, buf, dc, :], start=(dc == 0),
                                       stop=(dc == NDC - 1))
                    mGU.inc(ins, 1)
                    SC.wait_ge(mGU.h, mGU.v)
                    SC.wait_ge(eH.h, eH.v)
                    sSIL.inc(SC.activation(gate_sb[:], psU1[:], AF.Silu,
                                           scale=sc2[:, tc, 2:3]), 1)
                    psg[0] = sSIL.v
                    hpar = (grp * 32 + ic * 4 + tl) % 2
                    V.wait_ge(sSIL.h, sSIL.v)
                    V.wait_ge(dH[hpar].h, dH[hpar].v)
                    eH.inc(V.scalar_tensor_tensor(h_sb[:, hpar, :], psU2[:],
                                                  sc2[:, tc, 3:4], gate_sb[:],
                                                  A.mult, A.mult), 1)
                    psg[1] = eH.v
                    G.wait_ge(eH.h, eH.v)
                    dH[hpar].inc(G.dma_start(
                        out=s_h[tc * 128:(tc + 1) * 128,
                                ic * 512:(ic + 1) * 512],
                        in_=h_sb[:, hpar, :]), 16)
                gu_gate[buf] = mGU.v
                if drain:
                    drain_stair(3)
            xq_gate[xbuf] = mGU.v

        def mk_stair_pieces(grp):
            snap0, snap1 = dH[0].v, dH[1].v
            qw = INNER // 4

            def pro(tc):
                row = slice(tc * 128, (tc + 1) * 128)
                SC.wait_ge(dH[0].h, snap0)
                SC.wait_ge(dH[1].h, snap1)
                SC.wait_ge(eSTR.h, max(0, eSTR.v - 4))
                dHL.inc(SC.dma_start(out=hload[:, tc % 2, :],
                                     in_=s_h[row, :]), 16)

            def piece_a(tc, first):
                pro(tc)
                V.wait_ge(dHL.h, dHL.v)
                eRM.inc(V.tensor_reduce(rmax[:, 0, tc:tc + 1],
                                        hload[:, tc % 2, :],
                                        axis=AX.X, op=A.max,
                                        apply_absolute_value=True), 1)
                SC.wait_ge(eRM.h, eRM.v)
                SC.activation(rmax[:, 1, tc:tc + 1], rmax[:, 0, tc:tc + 1],
                              AF.Relu, bias=-1e-5)
                eCL.inc(SC.activation(rmax[:, 0, tc:tc + 1],
                                      rmax[:, 1, tc:tc + 1], AF.Identity,
                                      bias=EPS_LN), 1)
                V.wait_ge(eCL.h, eCL.v)
                eRC.inc(V.reciprocal(rmax[:, 1, tc:tc + 1],
                                     rmax[:, 0, tc:tc + 1]), 1)
                V.tensor_scalar(sc2[:, tc, 0:1], rmax[:, 0, tc:tc + 1],
                                th3[:, 2:3], 1.0 / 127.0, A.mult,
                                A.mult)
                SC.wait_ge(eRC.h, eRC.v)
                eAS.inc(SC.activation(sc2[:, tc, 1:2], rmax[:, 1, tc:tc + 1],
                                      AF.Identity, scale=127.0), 1)
                V.wait_ge(eAS.h, eAS.v)
                if tc >= 1:
                    pq = (tc - 1) // 4
                    V.wait_ge(dHQq[pq].h, dHQq[pq].v)

            def quarter(tc, qi):
                tl = tc % 4
                qb = (tc * 4 + qi) % 2

                def hsl(a, b):
                    return hload[:, tc % 2, qi * qw + a:qi * qw + b]
                rOB = r7O[:, 512:1024]
                remB = r7A[:, 512:1024]
                cbB = psSCb[:, 512:1024]
                SC.wait_ge(eAS.h, eAS.v)
                SC.wait_ge(eSTR.h, eSTR.v)
                SC.activation(rOB, hsl(512, 1024), AF.Identity,
                              scale=sc2[:, tc, 1:2], bias=128.5)
                SC.activation(cbB, rOB, AF.Sign, bias=-128.0)
                sSB.inc(SC.activation(cbB, cbB, AF.Relu, scale=128.0), 1)
                V.wait_ge(eAS.h, eAS.v)
                V.wait_ge(mTR.h, max(0, mTR.v - 1))
                V.tensor_scalar(r7O[:, 0:512], hsl(0, 512), sc2[:, tc, 1:2],
                                128.5, A.mult, A.add)
                curA = r7O
                prevB = 7
                for k in range(7, -1, -1):
                    step = float(1 << k)
                    V.tensor_scalar(c7[:, 0:512], curA[:, 0:512], step, step,
                                    A.is_ge, A.mult)
                    nxt = r7A if curA is r7O else curA
                    V.tensor_sub(nxt[:, 0:512], curA[:, 0:512], c7[:, 0:512])
                    curA = nxt
                    V.wait_ge(sSB.h, sSB.v)
                    srcB = rOB if prevB == 7 else remB
                    sVB.inc(V.tensor_sub(remB, srcB, cbB), 1)
                    if prevB > 0:
                        stb = float(1 << (prevB - 1))
                        SC.wait_ge(sVB.h, sVB.v)
                        SC.activation(cbB, remB, AF.Sign, bias=-stb)
                        sSB.inc(SC.activation(cbB, cbB, AF.Relu, scale=stb), 1)
                    prevB -= 1
                V.scalar_tensor_tensor(hqq[:, qb, 0:512], r7O[:, 0:512],
                                       -128.0, curA[:, 0:512],
                                       A.add, A.subtract)
                eSTR.inc(V.scalar_tensor_tensor(hqq[:, qb, 512:1024], rOB,
                                                -128.0, remB,
                                                A.add, A.subtract), 1)
                T.wait_ge(eSTR.h, eSTR.v)
                T.wait_ge(eTR.h, eTR.v)
                for j in range(8):
                    ins = T.matmul(psT[0][:, j * 128:(j + 1) * 128],
                                   hqq[:, qb, j * 128:(j + 1) * 128],
                                   identb[:], start=(j == 0),
                                   stop=(j == 7), is_transpose=True)
                mTR.inc(ins, 1)
                SC.wait_ge(mTR.h, mTR.v)
                eTR.inc(SC.activation(hqT[:, tl, qi * 8:(qi + 1) * 8, :],
                                      psT[0][:], AF.Identity), 1)
                if qi == 3:
                    tr_done[tc] = eTR.v

            for tl in range(4):
                tc = grp * 4 + tl
                stair_q.append(lambda tc=tc, tl=tl: piece_a(tc, tl == 0))
                for qi in range(4):
                    stair_q.append(lambda tc=tc, qi=qi: quarter(tc, qi))

        def down_group(g):
            for tl in range(4):
                tc = g * 4 + tl
                par = tc % 2
                row = slice(tc * 128, (tc + 1) * 128)
                if tc >= 1:
                    SY.wait_ge(eOUT.h, 2 * tc)
                dRL[par].inc(SY.dma_start(out=res_ld[:, 0, :],
                                          in_=s_res[row, :]), 16)
                for os_ in range(2):
                    ps = psDN[os_]
                    T.wait_ge(eOUT.h, dn_gate[os_])
                    T.wait_ge(eTR.h, tr_done[tc])
                    T.wait_ge(dWD.h, dWD.v)
                    for ic in range(NICD):
                        ins = T.matmul(
                            ps,
                            hqT[:, tl, ic, :],
                            wdT[:, ic, os_ * 512:(os_ + 1) * 512],
                            start=(ic == 0), stop=(ic == NICD - 1))
                    mDN.inc(ins, 1)
                    V.wait_ge(mDN.h, mDN.v)
                    V.wait_ge(dRL[par].h, dRL[par].v)
                    if tc >= 1 and os_ == 0:
                        V.wait_ge(dOUT[0].h, dOUT[0].v)
                    eOUT.inc(V.scalar_tensor_tensor(
                        out_sb[:, 0, os_ * 512:(os_ + 1) * 512], ps,
                        sc2[:, tc, 0:1],
                        res_ld[:, 0, os_ * 512:(os_ + 1) * 512],
                        A.mult, A.add), 1)
                    dn_gate[os_] = eOUT.v
                SC.wait_ge(eOUT.h, eOUT.v)
                dOUT[0].inc(SC.dma_start(out=out_hbm[row, :],
                                         in_=out_sb[:, 0, :]), 16)
                drain_stair(1)

        stair_q = []
        tr_done = {}
        u6_group(0)
        dWD.inc(SY.dma_start(out=wdT[:],
                             in_=s_wT["w_d"]
                                 .rearrange("(b p) c -> p b c", b=NICD)), 16)
        mk_stair_pieces(0)
        for grp in range(1, 4):
            u6_group(grp, drain=True)
            drain_stair(20)
            down_group(grp - 1)
            mk_stair_pieces(grp)
        drain_stair(20)
        down_group(3)
        drain_stair(20)

        G.wait_ge(dOUT[0].h, dOUT[0].v)
        G.wait_ge(dOUT[1].h, dOUT[1].v)
        scr_map = {"s_xn": s_xn, "s_xq": s_xq, "s_y": s_y, "s_h": s_h,
                   "s_hq": s_hq, "s_res": s_res, "s_w_g": s_w["w_g"],
                   "s_w_u": s_w["w_u"], "s_w_d": s_w["w_d"],
                   "th3": th3[:], "sc2": sc2[:]}
        ndbg = 0
        for dnm, src in scr_map.items():
            if dnm in dbg:
                dAW.inc(G.dma_start(out=dbg[dnm], in_=src), 16)
                ndbg += 1
        if ndbg:
            G.wait_ge(dAW.h, dAW.v)
        ctxG.close()

    @blk.sync
    def _(_sync):
        try:
            _body(_sync)
        except _Done:
            pass
        for st in (ctxG, ctxF, ctxC, ctxB, ctxQ, ctxU5, ctxPJ, ctxLN):
            st.close()

    ctx.close()
    return nc


def _host_consts():
    ind = np.zeros((NDC, 128, 16), dtype=ml_dtypes.bfloat16)
    indt = np.zeros((16, NDC, 128), dtype=np.float32)
    for dc in range(NDC):
        for p in range(128):
            hcol = 2 * dc + p // 64
            ind[dc, p, hcol] = 1
            indt[hcol, dc, p] = 1.0
    ones = np.ones((128, 128), dtype=np.float32)
    cmask = np.triu(np.ones((128, 128))).astype(ml_dtypes.bfloat16)
    ident = np.eye(128).astype(ml_dtypes.bfloat16)
    consts = np.zeros((128, 16), dtype=np.float32)
    for i, vv in enumerate(CONST_VALS):
        consts[:, i] = vv
    return ind, indt, ones, cmask, ident, consts


def kernel(L0_attn_x, L0_attn_w_q, L0_attn_w_k, L0_attn_w_v, L0_attn_w_o,
           L0_ffn_w_gate, L0_ffn_w_up, L0_ffn_w_down):
    nc = build_nc()
    ind, indt, ones, cmask, ident, consts = _host_consts()
    x = np.ascontiguousarray(np.asarray(L0_attn_x, dtype=np.float32))
    shared = {
        "w_q": np.ascontiguousarray(L0_attn_w_q, dtype=np.float32),
        "w_k": np.ascontiguousarray(L0_attn_w_k, dtype=np.float32),
        "w_v": np.ascontiguousarray(L0_attn_w_v, dtype=np.float32),
        "w_o": np.ascontiguousarray(L0_attn_w_o, dtype=np.float32),
        "w_g": np.ascontiguousarray(L0_ffn_w_gate, dtype=np.float32),
        "w_u": np.ascontiguousarray(L0_ffn_w_up, dtype=np.float32),
        "w_d": np.ascontiguousarray(L0_ffn_w_down, dtype=np.float32),
        "ind": ind, "indt": indt, "ones": ones, "cmask": cmask,
        "ident": ident,
        "consts": consts,
    }
    in_maps = [{"x": np.ascontiguousarray(x[i]), **shared} for i in range(8)]
    res = run_bass_kernel_spmd(nc, in_maps, core_ids=list(range(8)))
    return np.stack([res.results[i]["out"] for i in range(8)], axis=0)


# version 10
# speedup vs baseline: 1.0421x; 1.0327x over previous
import numpy as np
import ml_dtypes
import concourse.bass as bass
import concourse.mybir as mybir
from concourse.bass_utils import run_bass_kernel_spmd
from contextlib import ExitStack

F32 = mybir.dt.float32
BF16 = mybir.dt.bfloat16
A = mybir.AluOpType
AF = mybir.ActivationFunctionType
AX = mybir.AxisListType

S, D, H, DK, INNER = 2048, 1024, 16, 64, 4096
NTC = S // 128
NDC = D // 128
NIC = INNER // 512
NICD = INNER // 128
EPS_LN = 1e-5
EPS_RMS = 1.1920929e-07
SOFTCAP = 50.0

CONST_VALS = [EPS_LN, EPS_RMS, -8.0, -0.5e-5, 0.5e-5, -1e-5,
              128.5, -1.0, -2.0, -4.0, -16.0, -32.0, -64.0,
              -128.0]


class Sem:
    def __init__(self, ctx, nc, name):
        self.h = ctx.enter_context(nc.semaphore(name))
        self.v = 0

    def inc(self, instr, n=1):
        instr.then_inc(self.h, n)
        self.v += n
        return instr


def _stair(nc, out_bf, src_f32, scale_ap, rO, rA, cb, fd, done_sem=None):
    v = nc.vector
    v.tensor_scalar(rO[:, 0:fd], src_f32, scale_ap, 128.5, A.mult, A.add)
    cur = rO
    for k in range(7, -1, -1):
        step = float(1 << k)
        v.tensor_scalar(cb[:, 0:fd], cur[:, 0:fd], step, step, A.is_ge, A.mult)
        nxt = rA if cur is rO else cur
        v.tensor_sub(nxt[:, 0:fd], cur[:, 0:fd], cb[:, 0:fd])
        cur = nxt
    ins = v.scalar_tensor_tensor(out_bf, rO[:, 0:fd], -128.0, cur[:, 0:fd],
                                 A.add, A.subtract)
    if done_sem is not None:
        done_sem.inc(ins, 1)


def build_nc(debug=(), stop_after=None, proj_mode=0):
    nc = bass.Bass("TRN2", target_bir_lowering=False, debug=False,
                   detect_race_conditions=False)

    x_in = nc.dram_tensor("x", [S, D], F32, kind="ExternalInput").ap()
    w_shapes = (("w_q", [D, D]), ("w_k", [D, D]), ("w_v", [D, D]),
                ("w_o", [D, D]), ("w_g", [INNER, D]), ("w_u", [INNER, D]),
                ("w_d", [D, INNER]))
    w_in = {nm: nc.dram_tensor(nm, sh, F32, kind="ExternalInput").ap()
            for nm, sh in w_shapes}
    ind_in = nc.dram_tensor("ind", [NDC, 128, 16], BF16, kind="ExternalInput").ap()
    indt_in = nc.dram_tensor("indt", [16, NDC, 128], F32, kind="ExternalInput").ap()
    ones_in = nc.dram_tensor("ones", [128, 128], F32, kind="ExternalInput").ap()
    cmask_in = nc.dram_tensor("cmask", [128, 128], BF16, kind="ExternalInput").ap()
    ident_in = nc.dram_tensor("ident", [128, 128], BF16, kind="ExternalInput").ap()
    consts_in = nc.dram_tensor("consts", [128, 16], F32, kind="ExternalInput").ap()
    out_hbm = nc.dram_tensor("out", [S, D], F32, kind="ExternalOutput").ap()
    dbg = {}
    for name, shape, dt in debug:
        dbg[name] = nc.dram_tensor("dbg_" + name, shape, dt,
                                   kind="ExternalOutput").ap()

    s_w = {nm: nc.dram_tensor("s_" + nm, sh, BF16, kind="Internal").ap()
           for nm, sh in w_shapes}
    s_wT = {"w_g": nc.dram_tensor("s_wT_g", [D, INNER], BF16, kind="Internal").ap(),
            "w_u": nc.dram_tensor("s_wT_u", [D, INNER], BF16, kind="Internal").ap(),
            "w_d": nc.dram_tensor("s_wT_d", [INNER, D], BF16, kind="Internal").ap()}
    s_xn = nc.dram_tensor("s_xn", [S, D], BF16, kind="Internal").ap()
    s_xq = nc.dram_tensor("s_xq", [S, D], BF16, kind="Internal").ap()
    s_y = nc.dram_tensor("s_y", [S, D], BF16, kind="Internal").ap()
    s_h = nc.dram_tensor("s_h", [S, INNER], F32, kind="Internal").ap()
    s_hq = nc.dram_tensor("s_hq", [S, INNER], BF16, kind="Internal").ap()
    s_res = nc.dram_tensor("s_res", [S, D], F32, kind="Internal").ap()

    ctx = ExitStack()
    ctxLN = ExitStack()
    ctxPJ = ExitStack()
    ctxU5 = ExitStack()
    ctxQ = ExitStack()
    ctxB = ExitStack()
    ctxC = ExitStack()
    ctxF = ExitStack()
    ctxG = ExitStack()
    sb = lambda st, name, shape, dt: st.enter_context(nc.sbuf_tensor(name, shape, dt))
    sbr = lambda st, name, shape, dt: st.enter_context(nc.sbuf_tensor(name, shape, dt, side="right"))

    cst = sb(ctx, "cst", [128, 16], F32)
    ind = sb(ctx, "indb", [128, NDC, 16], BF16)
    ones = sb(ctx, "onesb", [128, 128], F32)
    cmask = sb(ctx, "cmaskb", [128, 128], BF16)
    identb = sb(ctx, "identb", [128, 128], BF16)
    stats = sb(ctx, "stats", [128, NTC, 10], F32)
    th3 = sb(ctx, "th3", [128, 8], F32)
    rmax = sb(ctx, "rmax", [128, 2, NTC], F32)
    sc2 = sb(ctx, "sc2", [128, NTC, 4], F32)

    for i, vv in enumerate(CONST_VALS):
        nc.const_aps.register(F32, vv, cst[:, i:i + 1])

    wstg = sb(ctxU5, "wstg", [128, 2, D], F32)
    accA = sb(ctxU5, "accA", [128, 3, 32], F32)
    sgA = sb(ctxU5, "sgA", [128, 2, D], BF16)
    sgB = sb(ctxU5, "sgB", [128, 2, D], BF16)
    w3o = sb(ctxU5, "w3o", [128, 2, D], BF16)
    wte = sb(ctxU5, "wte", [128, 2, D], BF16)

    xld = sb(ctxLN, "xld", [128, 2, D], F32)
    xc2 = sb(ctxLN, "xc2", [128, 2, D], F32)
    xsq = sb(ctxLN, "xsq", [128, D], F32)
    stgn = sb(ctxLN, "stgn", [128, 2, D], BF16)
    stgq = sb(ctxLN, "stgq", [128, 2, D], BF16)
    rO = sb(ctxLN, "rO", [128, D], F32)
    rA = sb(ctxLN, "rA", [128, D], F32)
    cbf = sb(ctxLN, "cbf", [128, D], F32)

    xn_fm = sbr(ctxPJ, "xn_fm", [128, NDC, S], BF16)
    indt = sbr(ctxPJ, "indtb", [16, NDC, 128], F32)
    wT = sbr(ctxPJ, "wT", [128, NDC, D], BF16)
    qsq = sbr(ctxPJ, "qsq", [128, 2, 512], BF16)
    a_q = sbr(ctxPJ, "a_q", [16, S], F32)
    a_ln = sbr(ctxPJ, "a_ln", [16, 512], F32)

    psU1 = ctx.enter_context(nc.psum_tensor("psU1", [128, 512], F32))
    psU2 = ctx.enter_context(nc.psum_tensor("psU2", [128, 512], F32))
    psSCa = ctx.enter_context(nc.psum_tensor("psSCa", [128, 1024], F32))
    psSCb = ctx.enter_context(nc.psum_tensor("psSCb", [128, 1024], F32))
    psY = ctx.enter_context(nc.psum_tensor("psY", [128, 512], F32))
    ctxPSS = ExitStack()
    psSS = ctxPSS.enter_context(nc.psum_tensor("psSS", [16, 512], F32))
    psT = [None]

    all_sems = []

    def sm(name):
        x = Sem(ctx, nc, name)
        all_sems.append(x)
        return x
    dX = [sm("dXa"), sm("dXb")]
    dXN = [sm("dXNa"), sm("dXNb")]
    dXNL, dXQL = sm("dXNL"), sm("dXQL")
    dAW, dWT = sm("dAW"), sm("dWT")
    sLN1, sLN2, sLN3, sLN4, sLN5 = sm("sLN1"), sm("sLN2"), sm("sLN3"), sm("sLN4"), sm("sLN5")
    sA1, sA2, sA3, sA4 = sm("sA1"), sm("sA2"), sm("sA3"), sm("sA4")
    mPJ, sSQ, sCP, mSS, sA, mBC, eRMS = sm("mPJ"), sm("sSQ"), sm("sCP"), sm("mSS"), sm("sA"), sm("mBC"), sm("eRMS")
    dF1 = [sm("dF1a"), sm("dF1b")]
    dF2 = [sm("dF2a"), sm("dF2b")]
    dW3 = [sm("dW3a"), sm("dW3b")]
    sAB, mMean, eTH, sSG, eW3 = sm("sAB"), sm("mMean"), sm("eTH"), sm("sSG"), sm("eW3")
    mQK, sTNH, sEXP, eMSK, mPV, eREC, sYN = sm("mQK"), sm("sTNH"), sm("sEXP"), sm("eMSK"), sm("mPV"), sm("eREC"), sm("sYN")
    dGU = [sm("dGUa"), sm("dGUb")]
    dH = [sm("dHa"), sm("dHb")]
    mGU, sSIL, eH = sm("mGU"), sm("sSIL"), sm("eH")
    eH2, eAS, eAS2, eH3, eW3b, eRM = sm("eH2"), sm("eAS"), sm("eAS2"), sm("eH3"), sm("eW3b"), sm("eRM")
    eCL, eRC = sm("eCL"), sm("eRC")
    mTW, eWT, mTR, eTR = sm("mTW"), sm("eWT"), sm("mTR"), sm("eTR")
    sSB, sVB = sm("sSB"), sm("sVB")
    dY, dYL, dWO, mO, eRS = sm("dY"), sm("dYL"), sm("dWO"), sm("mO"), sm("eRS")
    dRS = [sm("dRSa"), sm("dRSb")]
    dXR = [sm("dXRa"), sm("dXRb")]
    dHL, eSTR, dHQL, dWD, mDN, eOUT = sm("dHL"), sm("eSTR"), sm("dHQL"), sm("dWD"), sm("mDN"), sm("eOUT")
    dHQq = [sm("dHQ0"), sm("dHQ1"), sm("dHQ2"), sm("dHQ3")]
    dRL = [sm("dRLa"), sm("dRLb")]
    dOUT = [sm("dOUTa"), sm("dOUTb")]

    blk = ctx.enter_context(nc.Block())

    class _Done(Exception):
        pass

    def _body(_sync):
        T, V, SC, G, SY = nc.tensor, nc.vector, nc.scalar, nc.gpsimd, nc.sync

        def probe_stop(tag):
            if stop_after == tag:
                for sx in all_sems:
                    if sx.v:
                        nc.gpsimd.wait_ge(sx.h, sx.v)
                if "stats" in dbg:
                    dAW.inc(nc.gpsimd.dma_start(out=dbg["stats"], in_=stats[:]), 16)
                if "s_xn" in dbg:
                    dAW.inc(nc.gpsimd.dma_start(out=dbg["s_xn"], in_=s_xn), 16)
                if "s_xq" in dbg:
                    dAW.inc(nc.gpsimd.dma_start(out=dbg["s_xq"], in_=s_xq), 16)
                nc.gpsimd.wait_ge(dAW.h, dAW.v)
                raise _Done

        dAW.inc(G.dma_start(out=cst[:], in_=consts_in), 16)
        for dc in range(NDC):
            dAW.inc(G.dma_start(out=ind[:, dc, :], in_=ind_in[dc]), 16)
        dAW.inc(G.dma_start(out=ones[:], in_=ones_in), 16)
        dAW.inc(G.dma_start(out=cmask[:], in_=cmask_in), 16)
        dAW.inc(G.dma_start(out=identb[:], in_=ident_in), 16)
        for nm in ("w_q", "w_k", "w_v", "w_o"):
            dAW.inc(G.dma_start(out=s_w[nm], in_=w_in[nm]), 16)

        WMETA = {"g": ("w_g", 0), "u": ("w_u", 1), "d": ("w_d", 2)}

        def u5_chunk_ap(nm, chi):
            if nm == "w_d":
                r, c = chi // 4, chi % 4
                return (slice(r * 128, (r + 1) * 128),
                        slice(c * 1024, (c + 1) * 1024))
            return (slice(chi * 128, (chi + 1) * 128), slice(0, D))

        def u5_s1(m, lo, hi):
            nm, widx = WMETA[m]
            for chi in range(lo, hi):
                par = chi % 2
                SY.wait_ge(sAB.h, max(0, sAB.v - 1))
                SY.wait_ge(sSG.h, sSG.v)
                dF1[par].inc(SY.dma_start(out=wstg[:, par, :],
                                     in_=w_in[nm][u5_chunk_ap(nm, chi)]), 16)
                SC.wait_ge(dF1[par].h, dF1[par].v)
                sAB.inc(SC.activation(sgA[:, par, :], wstg[:, par, :], AF.Abs,
                                      accum_out=accA[:, widx, chi:chi + 1]), 1)

        def u5_fin(m):
            _, col = WMETA[m]
            T.wait_ge(sAB.h, sAB.v)
            T.wait_ge(sCP.h, sCP.v)
            T.wait_ge(eW3b.h, eW3b.v)
            mMean.inc(T.matmul(psU1[:, 0:32], ones[:], accA[:, col, :],
                               start=True, stop=True), 1)
            V.wait_ge(mMean.h, mMean.v)
            eW3b.inc(V.tensor_reduce(th3[:, 4:5], psU1[:, 0:32], axis=AX.X,
                                     op=A.add), 1)
            SC.wait_ge(eW3b.h, eW3b.v)
            SC.activation(th3[:, 5:6], th3[:, 4:5], AF.Relu,
                          scale=0.5 / (INNER * D), bias=-0.5e-5)
            eTH.inc(SC.activation(th3[:, col:col + 1], th3[:, 5:6], AF.Identity,
                                  bias=0.5e-5), 1)

        def u5T_view(nm, chi):
            if nm == "w_d":
                r, c = chi // 4, chi % 4
                reg = s_wT[nm][c * 1024:(c + 1) * 1024, r * 128:(r + 1) * 128]
            else:
                reg = s_wT[nm][:, chi * 128:(chi + 1) * 128]
            return reg.rearrange("(b p) w -> p b w", b=8)

        w3_ready = {}

        def u5_s2a(m, lo, hi):
            nm, col = WMETA[m]
            for chi in range(lo, hi):
                par = chi % 2
                SY.wait_ge(sSG.h, max(0, sSG.v - 1))
                SY.wait_ge(sAB.h, sAB.v)
                dF2[par].inc(SY.dma_start(out=wstg[:, par, :],
                                     in_=w_in[nm][u5_chunk_ap(nm, chi)]), 16)
                V.wait_ge(dF2[par].h, dF2[par].v)
                V.wait_ge(eTH.h, eTH.v)
                V.wait_ge(eW3.h, max(0, eW3.v - 1))
                V.tensor_scalar(sgA[:, par, :], wstg[:, par, :],
                                th3[:, col:col + 1], 2.0, A.is_ge, A.mult)
                sSG.inc(V.tensor_scalar(sgB[:, par, :], wstg[:, par, :],
                                        th3[:, 5 + col:6 + col], 2.0,
                                        A.is_le, A.mult), 1)
                V.wait_ge(dW3[par].h, dW3[par].v)
                eW3.inc(V.tensor_sub(w3o[:, par, :], sgA[:, par, :],
                                     sgB[:, par, :]), 1)
                w3_ready[(m, chi)] = eW3.v

        def u5_s2b(m, lo, hi):
            nm, col = WMETA[m]
            for chi in range(lo, hi):
                par = chi % 2
                T.wait_ge(eW3.h, w3_ready[(m, chi)])
                T.wait_ge(eWT.h, eWT.v)
                for j in range(NDC):
                    ins = T.matmul(psT[0][:, j * 128:(j + 1) * 128],
                                   w3o[:, par, j * 128:(j + 1) * 128],
                                   identb[:], start=(j == 0), stop=(j == NDC - 1),
                                   is_transpose=True)
                mTW.inc(ins, 1)
                V.wait_ge(mTW.h, mTW.v)
                eWT.inc(V.tensor_copy(wte[:, par, :], psT[0][:]), 1)
                G.wait_ge(eWT.h, eWT.v)
                dW3[par].inc(G.dma_start(
                    out=u5T_view(nm, chi),
                    in_=wte[:, par, :].rearrange("p (b w) -> p b w", b=8)), 16)

        u5_plan = []
        for m in ("g", "u", "d"):
            for j in range(16):
                u5_plan.append(("s1", m, j * 2, (j + 1) * 2))
        s2a = [("s2a", m, j, j + 1) for m in ("g", "u", "d") for j in range(32)]
        s2b = [("s2b", m, j, j + 1) for m in ("g", "u", "d") for j in range(32)]
        for i in range(len(s2a)):
            u5_plan.append(s2a[i])
            if i >= 1:
                u5_plan.append(s2b[i - 1])
        u5_plan.append(s2b[-1])
        u5_pos = [0]

        def u5_pieces(n):
            for _ in range(n):
                if u5_pos[0] >= len(u5_plan):
                    return
                e = u5_plan[u5_pos[0]]
                u5_pos[0] += 1
                if e[0] == "s1":
                    u5_s1(e[1], e[2], e[3])
                elif e[0] == "fin":
                    u5_fin(e[1])
                elif e[0] == "s2a":
                    u5_s2a(e[1], e[2], e[3])
                else:
                    u5_s2b(e[1], e[2], e[3])


        for tc in range(NTC):
            par = tc % 2
            row = slice(tc * 128, (tc + 1) * 128)
            if tc >= 2:
                SY.wait_ge(sLN5.h, tc - 1)
            dX[par].inc(SY.dma_start(out=xld[:, par, :], in_=x_in[row, :]), 16)
            V.wait_ge(dX[par].h, dX[par].v)
            sLN1.inc(V.tensor_reduce(stats[:, tc, 0:1], xld[:, par, :],
                                     axis=AX.X, op=A.add, negate=True), 1)
            SC.wait_ge(sLN1.h, sLN1.v)
            sA1.inc(SC.activation(stats[:, tc, 1:2], stats[:, tc, 0:1],
                                  AF.Identity, scale=1.0 / D), 1)
            V.wait_ge(sA1.h, sA1.v)
            if tc >= 2:
                V.wait_ge(sLN5.h, tc - 1)
            sA2.inc(V.tensor_scalar(xc2[:, par, :], xld[:, par, :],
                                    stats[:, tc, 1:2], None, A.add), 1)
            SC.wait_ge(sA2.h, sA2.v)
            sLN2.inc(SC.activation(xsq[:], xc2[:, par, :], AF.Square), 1)
            V.wait_ge(sLN2.h, sLN2.v)
            V.tensor_reduce(stats[:, tc, 2:3], xsq[:], axis=AX.X, op=A.add)
            sLN3.inc(V.tensor_reduce(stats[:, tc, 3:4], xc2[:, par, :],
                                     axis=AX.X, op=A.max,
                                     apply_absolute_value=True), 1)
            SC.wait_ge(sLN3.h, sLN3.v)
            SC.activation(stats[:, tc, 4:5], stats[:, tc, 2:3], AF.Ln,
                          scale=1.0 / D, bias=EPS_LN)
            sLN4.inc(SC.activation(stats[:, tc, 5:6], stats[:, tc, 4:5],
                                   AF.Exp, scale=-0.5), 1)
            V.wait_ge(sLN4.h, sLN4.v)
            sA3.inc(V.tensor_scalar(stats[:, tc, 6:7], stats[:, tc, 3:4],
                                    stats[:, tc, 5:6], 1e-5, A.mult, A.max), 1)
            SC.wait_ge(sA3.h, sA3.v)
            SC.activation(stats[:, tc, 4:5], stats[:, tc, 6:7], AF.Ln)
            sA4.inc(SC.activation(stats[:, tc, 7:8], stats[:, tc, 4:5],
                                  AF.Exp, scale=-1.0), 1)
            V.wait_ge(sA4.h, sA4.v)
            V.tensor_scalar(stats[:, tc, 8:9], stats[:, tc, 7:8],
                            stats[:, tc, 5:6], 127.0, A.mult, A.mult)
            V.tensor_scalar(stats[:, tc, 9:10], stats[:, tc, 6:7], 1.0 / 127.0,
                            None, A.mult)
            if tc >= 2:
                V.wait_ge(dXN[par].h, dXN[par].v)
            V.tensor_scalar(stgn[:, par, :], xc2[:, par, :], stats[:, tc, 5:6],
                            None, A.mult)
            _stair(nc, stgq[:, par, :], xc2[:, par, :], stats[:, tc, 8:9],
                   rO, rA, cbf, D, done_sem=sLN5)
            G.wait_ge(sLN5.h, sLN5.v)
            dXN[par].inc(G.dma_start(out=s_xn[row, :], in_=stgn[:, par, :]), 16)
            dXN[par].inc(G.dma_start(out=s_xq[row, :], in_=stgq[:, par, :]), 16)
            u5_pieces(3)

        for m in ("g", "u", "d"):
            u5_fin(m)
        for i in range(3):
            ins = SC.activation(th3[:, 5 + i:6 + i], th3[:, i:i + 1],
                                AF.Identity, scale=-1.0)
        eTH.inc(ins, 1)
        T.wait_ge(eW3b.h, eW3b.v)

        probe_stop("u0")
        dAW.inc(G.dma_start(out=indt[0:16, :, :], in_=indt_in), 16)
        SY.wait_ge(dXN[0].h, dXN[0].v)
        SY.wait_ge(dXN[1].h, dXN[1].v)
        for dc in range(NDC):
            dXNL.inc(SY.dma_start(out=xn_fm[:, dc, :],
                                  in_=s_xn[:, dc * 128:(dc + 1) * 128],
                                  transpose=True), 16)
        ctxLN.close()

        q_fm = sb(ctxQ, "q_fm", [128, NDC, S], BF16)
        k_fm = sb(ctxQ, "k_fm", [128, NDC, S], BF16)
        v_ext = sb(ctxQ, "v_ext", [128, NTC, 16, 65], BF16)
        V.memset(v_ext[:], 1.0)

        evac_gate = {id(psU1): (0, 0), id(psU2): (0, 0)}

        def proj_qk(wscr, dst_fm):
            SY.wait_ge(dAW.h, dAW.v)
            SY.wait_ge(mPJ.h, mPJ.v)
            for dc in range(NDC):
                dWT.inc(SY.dma_start(out=wT[:, dc, :],
                                     in_=wscr[:, dc * 128:(dc + 1) * 128],
                                     transpose=True), 16)
            probe_stop("uqw")
            for ts in range(4):
                tsl = slice(ts * 512, (ts + 1) * 512)
                for oc in range(NDC):
                    ps = psU1 if oc % 2 == 0 else psU2
                    g1, g2 = evac_gate[id(ps)]
                    T.wait_ge(sSQ.h, g1)
                    T.wait_ge(sCP.h, g2)
                    T.wait_ge(dWT.h, dWT.v)
                    T.wait_ge(dXNL.h, dXNL.v)
                    for dc in range(NDC):
                        ins = T.matmul(ps[:], wT[:, dc, oc * 128:(oc + 1) * 128],
                                       xn_fm[:, dc, tsl], start=(dc == 0),
                                       stop=(dc == NDC - 1))
                    mPJ.inc(ins, 1)
                    par2 = oc % 2
                    SC.wait_ge(mPJ.h, mPJ.v)
                    sSQ.inc(SC.activation(qsq[:, par2, :], ps[:], AF.Square), 1)
                    V.wait_ge(sSQ.h, sSQ.v)
                    sCP.inc(V.tensor_copy(dst_fm[:, oc, tsl], ps[:]), 1)
                    evac_gate[id(ps)] = (sSQ.v, sCP.v)
                    probe_stop("uq0")
                    if proj_mode >= 2:
                        continue
                    T.wait_ge(sSQ.h, sSQ.v)
                    if oc == 0:
                        T.wait_ge(sA.h, sA.v)
                    ins2 = T.matmul(psSS[:], ind[:, oc, :], qsq[:, par2, :],
                                    start=(oc == 0), stop=(oc == NDC - 1))
                if proj_mode >= 2:
                    continue
                mSS.inc(ins2, 1)
                SC.wait_ge(mSS.h, mSS.v)
                SC.activation(a_ln[:], psSS[:], AF.Ln, scale=1.0 / DK,
                              bias=EPS_RMS)
                sA.inc(SC.activation(a_q[:, tsl], a_ln[:], AF.Exp, scale=-0.5), 1)
                if proj_mode >= 1:
                    continue
                for oc in range(NDC):
                    T.wait_ge(sA.h, sA.v)
                    T.wait_ge(eRMS.h, eRMS.v)
                    mBC.inc(T.matmul(psY[:], indt[0:16, oc, :], a_q[0:16, tsl],
                                     start=True, stop=True), 1)
                    V.wait_ge(mBC.h, mBC.v)
                    eRMS.inc(V.tensor_mul(dst_fm[:, oc, tsl], dst_fm[:, oc, tsl],
                                          psY[:]), 1)

        proj_qk(s_w["w_q"], q_fm)
        probe_stop("uq")
        proj_qk(s_w["w_k"], k_fm)

        SY.wait_ge(mPJ.h, mPJ.v)
        for dc in range(NDC):
            dWT.inc(SY.dma_start(out=wT[:, dc, :],
                                 in_=s_w["w_v"][:, dc * 128:(dc + 1) * 128],
                                 transpose=True), 16)
        for tc in range(NTC):
            for vs in range(2):
                ps = psU1 if (tc * 2 + vs) % 2 == 0 else psU2
                g1, g2 = evac_gate[id(ps)]
                T.wait_ge(sCP.h, g2)
                T.wait_ge(dWT.h, dWT.v)
                for dc in range(NDC):
                    ins = T.matmul(ps[:], xn_fm[:, dc, tc * 128:(tc + 1) * 128],
                                   wT[:, dc, vs * 512:(vs + 1) * 512],
                                   start=(dc == 0), stop=(dc == NDC - 1))
                mPJ.inc(ins, 1)
                V.wait_ge(mPJ.h, mPJ.v)
                sCP.inc(V.tensor_copy(
                    v_ext[:, tc, vs * 8:(vs + 1) * 8, 0:64],
                    ps[:].rearrange("p (h f) -> p h f", h=8)), 1)
                evac_gate[id(ps)] = (sSQ.v, sCP.v)

        ctxPJ.close()
        ctxPSS.close()
        psT[0] = ctx.enter_context(nc.psum_tensor("psT", [128, 1024], BF16))
        probe_stop("u3")
        y_tm = sbr(ctxB, "y_tm", [128, NTC, D], BF16)
        t1b = sbr(ctxB, "t1b", [128, 2, 1024], F32)
        p_sb = sbr(ctxB, "p_sb", [128, 2, 1024], BF16)
        den = sbr(ctxB, "den", [128, 4], F32)

        T.wait_ge(eRMS.h, eRMS.v)
        T.wait_ge(sCP.h, sCP.v)
        for dnm, dbuf in (("q_fm", q_fm), ("k_fm", k_fm), ("v_ext", v_ext)):
            if dnm in dbg:
                G.wait_ge(eRMS.h, eRMS.v)
                G.wait_ge(sCP.h, sCP.v)
                dAW.inc(G.dma_start(out=dbg[dnm], in_=dbuf[:]), 16)

        pair_idx = [0]
        sc_gate = [0, 0]
        p_gate = [0, 0]

        for h in range(H):
            oc = h // 2
            pr = (h % 2) * 64
            for qt in range(4):
                qsl = slice(qt * 512, (qt + 1) * 512)
                T.wait_ge(sYN.h, h * 4 + qt)
                nr = 4 * qt + 4
                for j in range(nr // 2):
                    r0, r1 = 2 * j, 2 * j + 1
                    par = pair_idx[0] % 2
                    ps = psSCa if par == 0 else psSCb
                    T.wait_ge(sTNH.h, sc_gate[par])
                    for rr, half in ((r0, 0), (r1, 1)):
                        ins = T.matmul(ps[:, half * 512:(half + 1) * 512],
                                       k_fm[pr:pr + 64, oc, rr * 128:(rr + 1) * 128],
                                       q_fm[pr:pr + 64, oc, qsl],
                                       start=True, stop=True)
                    mQK.inc(ins, 1)
                    SC.wait_ge(mQK.h, mQK.v)
                    SC.wait_ge(mPV.h, p_gate[par])
                    sTNH.inc(SC.activation(t1b[:, par, :], ps[:], AF.Tanh,
                                           scale=1.0 / (8.0 * SOFTCAP)), 1)
                    sc_gate[par] = sTNH.v
                    sEXP.inc(SC.activation(p_sb[:, par, :], t1b[:, par, :],
                                           AF.Exp, scale=SOFTCAP, bias=-8.0), 1)
                    nmask = 0
                    for rr, half in ((r0, 0), (r1, 1)):
                        if 4 * qt <= rr:
                            V.wait_ge(sEXP.h, sEXP.v)
                            sl = half * 512 + (rr - 4 * qt) * 128
                            eMSK.inc(V.tensor_mul(p_sb[:, par, sl:sl + 128],
                                                  p_sb[:, par, sl:sl + 128],
                                                  cmask[:]), 1)
                            nmask += 1
                    T.wait_ge(sEXP.h, sEXP.v)
                    if nmask:
                        T.wait_ge(eMSK.h, eMSK.v)
                    for rr, half in ((r0, 0), (r1, 1)):
                        for cl in range(4):
                            c = 4 * qt + cl
                            if c < rr:
                                continue
                            ins = T.matmul(psY[:, cl * 128:cl * 128 + 65],
                                           p_sb[:, par, half * 512 + cl * 128:
                                                half * 512 + (cl + 1) * 128],
                                           v_ext[:, rr, h, :],
                                           start=(rr == 0 and cl == 0),
                                           stop=(rr == 4 * qt + 3 and cl == 3))
                    mPV.inc(ins, 1)
                    p_gate[par] = mPV.v
                    pair_idx[0] += 1
                V.wait_ge(mPV.h, mPV.v)
                for cl in range(4):
                    ins = V.reciprocal(den[:, cl:cl + 1],
                                       psY[:, cl * 128 + 64:cl * 128 + 65])
                eREC.inc(ins, 1)
                SC.wait_ge(eREC.h, eREC.v)
                for cl in range(4):
                    ins = SC.activation(y_tm[:, 4 * qt + cl, h * 64:(h + 1) * 64],
                                        psY[:, cl * 128:cl * 128 + 64],
                                        AF.Identity, scale=den[:, cl:cl + 1])
                sYN.inc(ins, 1)
                u5_pieces(3)
        u5_pieces(100)

        probe_stop("u4")
        ctxQ.close()
        ctxU5.close()

        y_fm = sb(ctxC, "y_fm", [128, NDC, S], BF16)
        woT = sb(ctxC, "woT", [128, NDC, D], BF16)
        x_re = sb(ctxC, "x_re", [128, 2, D], F32)
        res_sb = sb(ctxC, "res_sb", [128, 2, D], F32)

        G.wait_ge(sYN.h, sYN.v)
        if "y_tm" in dbg:
            dAW.inc(G.dma_start(out=dbg["y_tm"], in_=y_tm[:]), 16)
        for tc in range(NTC):
            dY.inc(G.dma_start(out=s_y[tc * 128:(tc + 1) * 128, :],
                               in_=y_tm[:, tc, :]), 16)
        SY.wait_ge(dY.h, dY.v)
        SY.wait_ge(dW3[0].h, dW3[0].v)
        SY.wait_ge(dW3[1].h, dW3[1].v)
        for dc in range(NDC):
            dYL.inc(SY.dma_start(out=y_fm[:, dc, :],
                                 in_=s_y[:, dc * 128:(dc + 1) * 128],
                                 transpose=True), 16)
            dWO.inc(SY.dma_start(out=woT[:, dc, :],
                                 in_=s_w["w_o"][:, dc * 128:(dc + 1) * 128],
                                 transpose=True), 16)
        T.wait_ge(eTH.h, eTH.v)
        res_gate = {id(psU1): 0, id(psU2): 0}
        for tc in range(NTC):
            par = tc % 2
            row = slice(tc * 128, (tc + 1) * 128)
            if tc >= 2:
                SY.wait_ge(eRS.h, 2 * (tc - 1))
            dXR[par].inc(SY.dma_start(out=x_re[:, par, :], in_=x_in[row, :]), 16)
            for os_ in range(2):
                ps = psU1 if os_ == 0 else psU2
                T.wait_ge(eRS.h, res_gate[id(ps)])
                T.wait_ge(dYL.h, dYL.v)
                T.wait_ge(dWO.h, dWO.v)
                for dc in range(NDC):
                    ins = T.matmul(ps[:], y_fm[:, dc, tc * 128:(tc + 1) * 128],
                                   woT[:, dc, os_ * 512:(os_ + 1) * 512],
                                   start=(dc == 0), stop=(dc == NDC - 1))
                mO.inc(ins, 1)
                V.wait_ge(mO.h, mO.v)
                V.wait_ge(dXR[par].h, dXR[par].v)
                if tc >= 2 and os_ == 0:
                    V.wait_ge(dRS[par].h, dRS[par].v)
                eRS.inc(V.tensor_add(res_sb[:, par, os_ * 512:(os_ + 1) * 512],
                                     ps[:],
                                     x_re[:, par, os_ * 512:(os_ + 1) * 512]), 1)
                res_gate[id(ps)] = eRS.v
            G.wait_ge(eRS.h, eRS.v)
            dRS[par].inc(G.dma_start(out=s_res[row, :], in_=res_sb[:, par, :]), 16)
        ctxB.close()
        ctxC.close()

        probe_stop("u8")
        xqg = sb(ctxF, "xqg", [128, 2, NDC, 512], BF16)
        gusl = sb(ctxF, "gusl", [128, 2, 2, NDC, 512], BF16)
        gate_sb = sb(ctxF, "gate_sb", [128, 512], F32)
        h_sb = sb(ctxF, "h_sb", [128, 2, 512], F32)
        hload = sb(ctxF, "hload", [128, 2, INNER], F32)
        r7O = sb(ctxF, "r7O", [128, INNER // 4], F32)
        r7A = sb(ctxF, "r7A", [128, INNER // 4], F32)
        c7s = sb(ctxF, "c7s", [128, 512], F32)
        hqq = sb(ctxF, "hqq", [128, 2, 1024], BF16)
        wdT = sb(ctxF, "wdT", [128, NICD, D], BF16)
        hqT = sb(ctxF, "hqT", [128, 4, NICD, 128], BF16)
        res_ld = sb(ctxF, "res_ld", [128, 1, D], F32)
        out_sb = sb(ctxF, "out_sb", [128, 1, D], F32)

        V.wait_ge(eTH.h, eTH.v)
        for tc in range(NTC):
            V.tensor_scalar(sc2[:, tc, 2:3], stats[:, tc, 9:10],
                            th3[:, 0:1], None, A.mult)
            V.tensor_scalar(sc2[:, tc, 3:4], stats[:, tc, 9:10],
                            th3[:, 1:2], None, A.mult)
        SY.wait_ge(sYN.h, 4 * H)
        SY.wait_ge(dY.h, dY.v)
        SY.wait_ge(dRS[0].h, dRS[0].v)
        SY.wait_ge(dRS[1].h, dRS[1].v)
        SY.wait_ge(dW3[0].h, dW3[0].v)
        SY.wait_ge(dW3[1].h, dW3[1].v)
        T.wait_ge(eRS.h, eRS.v)
        T.wait_ge(sTNH.h, sTNH.v)
        gu_gate = [0, 0]
        xq_gate = [0, 0]
        psg1 = [0, 0]
        psg2 = [0, 0]
        dn_gate = [0, 0]
        psDN = (psSCa[:, 0:512], psSCa[:, 512:1024])
        psG = (psU1[:], psSCb[:, 0:512])
        psU = (psU2[:], psY[:])
        dXQg = [dXQL, dYL]

        def drain_stair(n=1):
            for _ in range(n):
                if stair_q:
                    stair_q.pop(0)()

        def u6_group(grp, drain=False):
            xbuf = grp % 2
            SY.wait_ge(mGU.h, xq_gate[xbuf])
            for dc in range(NDC):
                dXQg[xbuf].inc(SY.dma_start(
                    out=xqg[:, xbuf, dc, :],
                    in_=s_xq[grp * 512:(grp + 1) * 512,
                             dc * 128:(dc + 1) * 128],
                    transpose=True), 16)
            for ic in range(NIC):
                buf = (grp * NIC + ic) % 2
                SY.wait_ge(mGU.h, gu_gate[buf])
                for w, nm in ((0, "w_g"), (1, "w_u")):
                    dGU[buf].inc(SY.dma_start(
                        out=gusl[:, w, buf, :, :],
                        in_=s_wT[nm][:, ic * 512:(ic + 1) * 512]
                            .rearrange("(b p) c -> p b c", b=NDC)), 16)
                for tl in range(4):
                    tc = grp * 4 + tl
                    unit = grp * 32 + ic * 4 + tl
                    ub = unit % 2
                    T.wait_ge(dGU[buf].h, dGU[buf].v)
                    T.wait_ge(dXQg[xbuf].h, dXQg[xbuf].v)
                    T.wait_ge(sSIL.h, psg1[ub])
                    for dc in range(NDC):
                        T.matmul(psG[ub],
                                 xqg[:, xbuf, dc, tl * 128:(tl + 1) * 128],
                                 gusl[:, 0, buf, dc, :], start=(dc == 0),
                                 stop=(dc == NDC - 1))
                    T.wait_ge(eH.h, psg2[ub])
                    for dc in range(NDC):
                        ins = T.matmul(psU[ub],
                                       xqg[:, xbuf, dc, tl * 128:(tl + 1) * 128],
                                       gusl[:, 1, buf, dc, :], start=(dc == 0),
                                       stop=(dc == NDC - 1))
                    mGU.inc(ins, 1)
                    SC.wait_ge(mGU.h, mGU.v)
                    SC.wait_ge(eH.h, eH.v)
                    sSIL.inc(SC.activation(gate_sb[:], psG[ub], AF.Silu,
                                           scale=sc2[:, tc, 2:3]), 1)
                    psg1[ub] = sSIL.v
                    hpar = unit % 2
                    V.wait_ge(sSIL.h, sSIL.v)
                    V.wait_ge(dH[hpar].h, dH[hpar].v)
                    eH.inc(V.scalar_tensor_tensor(h_sb[:, hpar, :], psU[ub],
                                                  sc2[:, tc, 3:4], gate_sb[:],
                                                  A.mult, A.mult), 1)
                    psg2[ub] = eH.v
                    G.wait_ge(eH.h, eH.v)
                    dH[hpar].inc(G.dma_start(
                        out=s_h[tc * 128:(tc + 1) * 128,
                                ic * 512:(ic + 1) * 512],
                        in_=h_sb[:, hpar, :]), 16)
                gu_gate[buf] = mGU.v
                if drain:
                    drain_stair(3)
            xq_gate[xbuf] = mGU.v

        def mk_stair_pieces(grp):
            snap0, snap1 = dH[0].v, dH[1].v
            qw = INNER // 4

            def pro(tc):
                row = slice(tc * 128, (tc + 1) * 128)
                SC.wait_ge(dH[0].h, snap0)
                SC.wait_ge(dH[1].h, snap1)
                SC.wait_ge(eSTR.h, max(0, eSTR.v - 4))
                dHL.inc(SC.dma_start(out=hload[:, tc % 2, :],
                                     in_=s_h[row, :]), 16)

            def piece_a(tc, first):
                pro(tc)
                V.wait_ge(dHL.h, dHL.v)
                eRM.inc(V.tensor_reduce(rmax[:, 0, tc:tc + 1],
                                        hload[:, tc % 2, :],
                                        axis=AX.X, op=A.max,
                                        apply_absolute_value=True), 1)
                SC.wait_ge(eRM.h, eRM.v)
                SC.activation(rmax[:, 1, tc:tc + 1], rmax[:, 0, tc:tc + 1],
                              AF.Relu, bias=-1e-5)
                eCL.inc(SC.activation(rmax[:, 0, tc:tc + 1],
                                      rmax[:, 1, tc:tc + 1], AF.Identity,
                                      bias=EPS_LN), 1)
                V.wait_ge(eCL.h, eCL.v)
                eRC.inc(V.reciprocal(rmax[:, 1, tc:tc + 1],
                                     rmax[:, 0, tc:tc + 1]), 1)
                V.tensor_scalar(sc2[:, tc, 0:1], rmax[:, 0, tc:tc + 1],
                                th3[:, 2:3], 1.0 / 127.0, A.mult,
                                A.mult)
                SC.wait_ge(eRC.h, eRC.v)
                eAS.inc(SC.activation(sc2[:, tc, 1:2], rmax[:, 1, tc:tc + 1],
                                      AF.Identity, scale=127.0), 1)
                V.wait_ge(eAS.h, eAS.v)
                if tc >= 1:
                    pq = (tc - 1) // 4
                    V.wait_ge(dHQq[pq].h, dHQq[pq].v)

            def quarter(tc, qi):
                tl = tc % 4
                qb = (tc * 4 + qi) % 2

                def hsl(a, b):
                    return hload[:, tc % 2, qi * qw + a:qi * qw + b]
                rOB = r7O[:, 512:1024]
                remB = r7A[:, 512:1024]
                cbB = psSCb[:, 512:1024]
                SC.wait_ge(eAS.h, eAS.v)
                SC.wait_ge(eSTR.h, eSTR.v)
                SC.activation(rOB, hsl(512, 1024), AF.Identity,
                              scale=sc2[:, tc, 1:2], bias=128.5)
                SC.activation(cbB, rOB, AF.Sign, bias=-128.0)
                sSB.inc(SC.activation(cbB, cbB, AF.Relu, scale=128.0), 1)
                V.wait_ge(eAS.h, eAS.v)
                V.wait_ge(mTR.h, max(0, mTR.v - 1))
                V.tensor_scalar(r7O[:, 0:512], hsl(0, 512), sc2[:, tc, 1:2],
                                128.5, A.mult, A.add)
                curA = r7O
                prevB = 7
                for k in range(7, -1, -1):
                    step = float(1 << k)
                    V.tensor_scalar(c7s[:], curA[:, 0:512], step, step,
                                    A.is_ge, A.mult)
                    nxt = r7A if curA is r7O else curA
                    V.tensor_sub(nxt[:, 0:512], curA[:, 0:512], c7s[:])
                    curA = nxt
                    V.wait_ge(sSB.h, sSB.v)
                    srcB = rOB if prevB == 7 else remB
                    sVB.inc(V.tensor_sub(remB, srcB, cbB), 1)
                    if prevB > 0:
                        stb = float(1 << (prevB - 1))
                        SC.wait_ge(sVB.h, sVB.v)
                        SC.activation(cbB, remB, AF.Sign, bias=-stb)
                        sSB.inc(SC.activation(cbB, cbB, AF.Relu, scale=stb), 1)
                    prevB -= 1
                V.scalar_tensor_tensor(hqq[:, qb, 0:512], r7O[:, 0:512],
                                       -128.0, curA[:, 0:512],
                                       A.add, A.subtract)
                eSTR.inc(V.scalar_tensor_tensor(hqq[:, qb, 512:1024], rOB,
                                                -128.0, remB,
                                                A.add, A.subtract), 1)
                T.wait_ge(eSTR.h, eSTR.v)
                T.wait_ge(eTR.h, eTR.v)
                for j in range(8):
                    ins = T.matmul(psT[0][:, j * 128:(j + 1) * 128],
                                   hqq[:, qb, j * 128:(j + 1) * 128],
                                   identb[:], start=(j == 0),
                                   stop=(j == 7), is_transpose=True)
                mTR.inc(ins, 1)
                SC.wait_ge(mTR.h, mTR.v)
                eTR.inc(SC.activation(hqT[:, tl, qi * 8:(qi + 1) * 8, :],
                                      psT[0][:], AF.Identity), 1)
                if qi == 3:
                    tr_done[tc] = eTR.v

            for tl in range(4):
                tc = grp * 4 + tl
                stair_q.append(lambda tc=tc, tl=tl: piece_a(tc, tl == 0))
                for qi in range(4):
                    stair_q.append(lambda tc=tc, qi=qi: quarter(tc, qi))

        def down_group(g):
            for tl in range(4):
                tc = g * 4 + tl
                par = tc % 2
                row = slice(tc * 128, (tc + 1) * 128)
                if tc >= 1:
                    SY.wait_ge(eOUT.h, 2 * tc)
                dRL[par].inc(SY.dma_start(out=res_ld[:, 0, :],
                                          in_=s_res[row, :]), 16)
                for os_ in range(2):
                    ps = psDN[os_]
                    T.wait_ge(eOUT.h, dn_gate[os_])
                    T.wait_ge(eTR.h, tr_done[tc])
                    T.wait_ge(dWD.h, dWD.v)
                    for ic in range(NICD):
                        ins = T.matmul(
                            ps,
                            hqT[:, tl, ic, :],
                            wdT[:, ic, os_ * 512:(os_ + 1) * 512],
                            start=(ic == 0), stop=(ic == NICD - 1))
                    mDN.inc(ins, 1)
                    V.wait_ge(mDN.h, mDN.v)
                    V.wait_ge(dRL[par].h, dRL[par].v)
                    if tc >= 1 and os_ == 0:
                        V.wait_ge(dOUT[0].h, dOUT[0].v)
                    eOUT.inc(V.scalar_tensor_tensor(
                        out_sb[:, 0, os_ * 512:(os_ + 1) * 512], ps,
                        sc2[:, tc, 0:1],
                        res_ld[:, 0, os_ * 512:(os_ + 1) * 512],
                        A.mult, A.add), 1)
                    dn_gate[os_] = eOUT.v
                SC.wait_ge(eOUT.h, eOUT.v)
                dOUT[0].inc(SC.dma_start(out=out_hbm[row, :],
                                         in_=out_sb[:, 0, :]), 16)
                drain_stair(1)

        stair_q = []
        tr_done = {}
        u6_group(0)
        dWD.inc(SY.dma_start(out=wdT[:],
                             in_=s_wT["w_d"]
                                 .rearrange("(b p) c -> p b c", b=NICD)), 16)
        mk_stair_pieces(0)
        for grp in range(1, 4):
            u6_group(grp, drain=True)
            drain_stair(20)
            down_group(grp - 1)
            mk_stair_pieces(grp)
        drain_stair(20)
        down_group(3)
        drain_stair(20)

        G.wait_ge(dOUT[0].h, dOUT[0].v)
        G.wait_ge(dOUT[1].h, dOUT[1].v)
        scr_map = {"s_xn": s_xn, "s_xq": s_xq, "s_y": s_y, "s_h": s_h,
                   "s_hq": s_hq, "s_res": s_res, "s_w_g": s_w["w_g"],
                   "s_w_u": s_w["w_u"], "s_w_d": s_w["w_d"],
                   "th3": th3[:], "sc2": sc2[:]}
        ndbg = 0
        for dnm, src in scr_map.items():
            if dnm in dbg:
                dAW.inc(G.dma_start(out=dbg[dnm], in_=src), 16)
                ndbg += 1
        if ndbg:
            G.wait_ge(dAW.h, dAW.v)
        ctxG.close()

    @blk.sync
    def _(_sync):
        try:
            _body(_sync)
        except _Done:
            pass
        for st in (ctxG, ctxF, ctxC, ctxB, ctxQ, ctxU5, ctxPJ, ctxLN):
            st.close()

    ctx.close()
    return nc


def _host_consts():
    ind = np.zeros((NDC, 128, 16), dtype=ml_dtypes.bfloat16)
    indt = np.zeros((16, NDC, 128), dtype=np.float32)
    for dc in range(NDC):
        for p in range(128):
            hcol = 2 * dc + p // 64
            ind[dc, p, hcol] = 1
            indt[hcol, dc, p] = 1.0
    ones = np.ones((128, 128), dtype=np.float32)
    cmask = np.triu(np.ones((128, 128))).astype(ml_dtypes.bfloat16)
    ident = np.eye(128).astype(ml_dtypes.bfloat16)
    consts = np.zeros((128, 16), dtype=np.float32)
    for i, vv in enumerate(CONST_VALS):
        consts[:, i] = vv
    return ind, indt, ones, cmask, ident, consts


def kernel(L0_attn_x, L0_attn_w_q, L0_attn_w_k, L0_attn_w_v, L0_attn_w_o,
           L0_ffn_w_gate, L0_ffn_w_up, L0_ffn_w_down):
    nc = build_nc()
    ind, indt, ones, cmask, ident, consts = _host_consts()
    x = np.ascontiguousarray(np.asarray(L0_attn_x, dtype=np.float32))
    shared = {
        "w_q": np.ascontiguousarray(L0_attn_w_q, dtype=np.float32),
        "w_k": np.ascontiguousarray(L0_attn_w_k, dtype=np.float32),
        "w_v": np.ascontiguousarray(L0_attn_w_v, dtype=np.float32),
        "w_o": np.ascontiguousarray(L0_attn_w_o, dtype=np.float32),
        "w_g": np.ascontiguousarray(L0_ffn_w_gate, dtype=np.float32),
        "w_u": np.ascontiguousarray(L0_ffn_w_up, dtype=np.float32),
        "w_d": np.ascontiguousarray(L0_ffn_w_down, dtype=np.float32),
        "ind": ind, "indt": indt, "ones": ones, "cmask": cmask,
        "ident": ident,
        "consts": consts,
    }
    in_maps = [{"x": np.ascontiguousarray(x[i]), **shared} for i in range(8)]
    res = run_bass_kernel_spmd(nc, in_maps, core_ids=list(range(8)))
    return np.stack([res.results[i]["out"] for i in range(8)], axis=0)


# version 11
# speedup vs baseline: 1.0875x; 1.0435x over previous
import numpy as np
import ml_dtypes
import concourse.bass as bass
import concourse.mybir as mybir
from concourse.bass_utils import run_bass_kernel_spmd
from contextlib import ExitStack

F32 = mybir.dt.float32
BF16 = mybir.dt.bfloat16
A = mybir.AluOpType
AF = mybir.ActivationFunctionType
AX = mybir.AxisListType

S, D, H, DK, INNER = 2048, 1024, 16, 64, 4096
NTC = S // 128
NDC = D // 128
NIC = INNER // 512
NICD = INNER // 128
EPS_LN = 1e-5
EPS_RMS = 1.1920929e-07
SOFTCAP = 50.0

CONST_VALS = [EPS_LN, EPS_RMS, -8.0, -0.5e-5, 0.5e-5, -1e-5,
              128.5, -1.0, -2.0, -4.0, -16.0, -32.0, -64.0,
              -128.0]


class Sem:
    def __init__(self, ctx, nc, name):
        self.h = ctx.enter_context(nc.semaphore(name))
        self.v = 0

    def inc(self, instr, n=1):
        instr.then_inc(self.h, n)
        self.v += n
        return instr


def _stair(nc, out_bf, src_f32, scale_ap, rO, rA, cb, fd, done_sem=None):
    v = nc.vector
    v.tensor_scalar(rO[:, 0:fd], src_f32, scale_ap, 128.5, A.mult, A.add)
    cur = rO
    for k in range(7, -1, -1):
        step = float(1 << k)
        v.tensor_scalar(cb[:, 0:fd], cur[:, 0:fd], step, step, A.is_ge, A.mult)
        nxt = rA if cur is rO else cur
        v.tensor_sub(nxt[:, 0:fd], cur[:, 0:fd], cb[:, 0:fd])
        cur = nxt
    ins = v.scalar_tensor_tensor(out_bf, rO[:, 0:fd], -128.0, cur[:, 0:fd],
                                 A.add, A.subtract)
    if done_sem is not None:
        done_sem.inc(ins, 1)


def build_nc(debug=(), stop_after=None, proj_mode=0):
    nc = bass.Bass("TRN2", target_bir_lowering=False, debug=False,
                   detect_race_conditions=False)

    x_in = nc.dram_tensor("x", [S, D], F32, kind="ExternalInput").ap()
    w_shapes = (("w_q", [D, D]), ("w_k", [D, D]), ("w_v", [D, D]),
                ("w_o", [D, D]), ("w_g", [INNER, D]), ("w_u", [INNER, D]),
                ("w_d", [D, INNER]))
    w_in = {nm: nc.dram_tensor(nm, sh, F32, kind="ExternalInput").ap()
            for nm, sh in w_shapes}
    ind_in = nc.dram_tensor("ind", [NDC, 128, 16], BF16, kind="ExternalInput").ap()
    indt_in = nc.dram_tensor("indt", [16, NDC, 128], F32, kind="ExternalInput").ap()
    ones_in = nc.dram_tensor("ones", [128, 128], F32, kind="ExternalInput").ap()
    cmask_in = nc.dram_tensor("cmask", [128, 128], BF16, kind="ExternalInput").ap()
    ident_in = nc.dram_tensor("ident", [128, 128], BF16, kind="ExternalInput").ap()
    consts_in = nc.dram_tensor("consts", [128, 16], F32, kind="ExternalInput").ap()
    out_hbm = nc.dram_tensor("out", [S, D], F32, kind="ExternalOutput").ap()
    dbg = {}
    for name, shape, dt in debug:
        dbg[name] = nc.dram_tensor("dbg_" + name, shape, dt,
                                   kind="ExternalOutput").ap()

    s_w = {nm: nc.dram_tensor("s_" + nm, sh, BF16, kind="Internal").ap()
           for nm, sh in w_shapes}
    s_wT = {"w_g": nc.dram_tensor("s_wT_g", [D, INNER], BF16, kind="Internal").ap(),
            "w_u": nc.dram_tensor("s_wT_u", [D, INNER], BF16, kind="Internal").ap(),
            "w_d": nc.dram_tensor("s_wT_d", [INNER, D], BF16, kind="Internal").ap()}
    s_xn = nc.dram_tensor("s_xn", [S, D], BF16, kind="Internal").ap()
    s_xq = nc.dram_tensor("s_xq", [S, D], BF16, kind="Internal").ap()
    s_y = nc.dram_tensor("s_y", [S, D], BF16, kind="Internal").ap()
    s_h = nc.dram_tensor("s_h", [S, INNER], F32, kind="Internal").ap()
    s_hq = nc.dram_tensor("s_hq", [S, INNER], BF16, kind="Internal").ap()
    s_res = nc.dram_tensor("s_res", [S, D], F32, kind="Internal").ap()

    ctx = ExitStack()
    ctxLN = ExitStack()
    ctxPJ = ExitStack()
    ctxU5 = ExitStack()
    ctxQ = ExitStack()
    ctxB = ExitStack()
    ctxC = ExitStack()
    ctxF = ExitStack()
    ctxG = ExitStack()
    sb = lambda st, name, shape, dt: st.enter_context(nc.sbuf_tensor(name, shape, dt))
    sbr = lambda st, name, shape, dt: st.enter_context(nc.sbuf_tensor(name, shape, dt, side="right"))

    cst = sb(ctx, "cst", [128, 16], F32)
    ind = sb(ctx, "indb", [128, NDC, 16], BF16)
    ones = sb(ctx, "onesb", [128, 128], F32)
    cmask = sb(ctx, "cmaskb", [128, 128], BF16)
    identb = sb(ctx, "identb", [128, 128], BF16)
    stats = sb(ctx, "stats", [128, NTC, 10], F32)
    th3 = sb(ctx, "th3", [128, 8], F32)
    rmax = sb(ctx, "rmax", [128, 2, NTC], F32)
    sc2 = sb(ctx, "sc2", [128, NTC, 4], F32)

    for i, vv in enumerate(CONST_VALS):
        nc.const_aps.register(F32, vv, cst[:, i:i + 1])

    wstg = sb(ctxU5, "wstg", [128, 2, D], F32)
    accA = sb(ctxU5, "accA", [128, 3, 32], F32)
    sgA = sb(ctxU5, "sgA", [128, 2, D], BF16)
    sgB = sb(ctxU5, "sgB", [128, 2, D], BF16)
    w3o = sb(ctxU5, "w3o", [128, 2, D], BF16)
    wte = sb(ctxU5, "wte", [128, 2, D], BF16)

    xld = sb(ctxLN, "xld", [128, 2, D], F32)
    xc2 = sb(ctxLN, "xc2", [128, 2, D], F32)
    xsq = sb(ctxLN, "xsq", [128, D], F32)
    stgn = sb(ctxLN, "stgn", [128, 2, D], BF16)
    stgq = sb(ctxLN, "stgq", [128, 2, D], BF16)
    rO = sb(ctxLN, "rO", [128, D], F32)
    rA = sb(ctxLN, "rA", [128, D], F32)
    cbf = sb(ctxLN, "cbf", [128, D], F32)

    xn_fm = sbr(ctxPJ, "xn_fm", [128, NDC, S], BF16)
    indt = sbr(ctxPJ, "indtb", [16, NDC, 128], F32)
    wT = sbr(ctxPJ, "wT", [128, NDC, D], BF16)
    qsq = sbr(ctxPJ, "qsq", [128, 2, 512], BF16)
    a_q = sbr(ctxPJ, "a_q", [16, S], F32)
    a_ln = sbr(ctxPJ, "a_ln", [16, 512], F32)

    psU1 = ctx.enter_context(nc.psum_tensor("psU1", [128, 512], F32))
    psU2 = ctx.enter_context(nc.psum_tensor("psU2", [128, 512], F32))
    psSCa = ctx.enter_context(nc.psum_tensor("psSCa", [128, 1024], F32))
    psSCb = ctx.enter_context(nc.psum_tensor("psSCb", [128, 1024], F32))
    psY = ctx.enter_context(nc.psum_tensor("psY", [128, 512], F32))
    ctxPSS = ExitStack()
    psSS = ctxPSS.enter_context(nc.psum_tensor("psSS", [16, 512], F32))
    psT = [None]

    all_sems = []

    def sm(name):
        x = Sem(ctx, nc, name)
        all_sems.append(x)
        return x
    dX = [sm("dXa"), sm("dXb")]
    dXN = [sm("dXNa"), sm("dXNb")]
    dXNL, dXQL = sm("dXNL"), sm("dXQL")
    dAW, dWT = sm("dAW"), sm("dWT")
    sLN1, sLN2, sLN3, sLN4, sLN5 = sm("sLN1"), sm("sLN2"), sm("sLN3"), sm("sLN4"), sm("sLN5")
    sA1, sA2, sA3, sA4 = sm("sA1"), sm("sA2"), sm("sA3"), sm("sA4")
    mPJ, sSQ, sCP, mSS, sA, mBC, eRMS = sm("mPJ"), sm("sSQ"), sm("sCP"), sm("mSS"), sm("sA"), sm("mBC"), sm("eRMS")
    dF1 = [sm("dF1a"), sm("dF1b")]
    dF2 = [sm("dF2a"), sm("dF2b")]
    dW3 = [sm("dW3a"), sm("dW3b")]
    sAB, mMean, eTH, sSG, eW3 = sm("sAB"), sm("mMean"), sm("eTH"), sm("sSG"), sm("eW3")
    mQK, sTNH, sEXP, eMSK, mPV, eREC, sYN = sm("mQK"), sm("sTNH"), sm("sEXP"), sm("eMSK"), sm("mPV"), sm("eREC"), sm("sYN")
    dGU = [sm("dGUa"), sm("dGUb")]
    dH = [sm("dHa"), sm("dHb")]
    mGU, sSIL, eH = sm("mGU"), sm("sSIL"), sm("eH")
    eH2, eAS, eAS2, eH3, eW3b, eRM = sm("eH2"), sm("eAS"), sm("eAS2"), sm("eH3"), sm("eW3b"), sm("eRM")
    eCL, eRC = sm("eCL"), sm("eRC")
    mTW, eWT, mTR, eTR = sm("mTW"), sm("eWT"), sm("mTR"), sm("eTR")
    sSB, sVB = sm("sSB"), sm("sVB")
    dY, dYL, dWO, mO, eRS = sm("dY"), sm("dYL"), sm("dWO"), sm("mO"), sm("eRS")
    dRS = [sm("dRSa"), sm("dRSb")]
    dXR = [sm("dXRa"), sm("dXRb")]
    dHL, eSTR, dHQL, dWD, mDN, eOUT = sm("dHL"), sm("eSTR"), sm("dHQL"), sm("dWD"), sm("mDN"), sm("eOUT")
    dHQq = [sm("dHQ0"), sm("dHQ1"), sm("dHQ2"), sm("dHQ3")]
    dRL = [sm("dRLa"), sm("dRLb")]
    dOUT = [sm("dOUTa"), sm("dOUTb")]

    blk = ctx.enter_context(nc.Block())

    class _Done(Exception):
        pass

    def _body(_sync):
        T, V, SC, G, SY = nc.tensor, nc.vector, nc.scalar, nc.gpsimd, nc.sync

        def probe_stop(tag):
            if stop_after == tag:
                for sx in all_sems:
                    if sx.v:
                        nc.gpsimd.wait_ge(sx.h, sx.v)
                if "stats" in dbg:
                    dAW.inc(nc.gpsimd.dma_start(out=dbg["stats"], in_=stats[:]), 16)
                if "s_xn" in dbg:
                    dAW.inc(nc.gpsimd.dma_start(out=dbg["s_xn"], in_=s_xn), 16)
                if "s_xq" in dbg:
                    dAW.inc(nc.gpsimd.dma_start(out=dbg["s_xq"], in_=s_xq), 16)
                nc.gpsimd.wait_ge(dAW.h, dAW.v)
                raise _Done

        dAW.inc(G.dma_start(out=cst[:], in_=consts_in), 16)
        for dc in range(NDC):
            dAW.inc(G.dma_start(out=ind[:, dc, :], in_=ind_in[dc]), 16)
        dAW.inc(G.dma_start(out=ones[:], in_=ones_in), 16)
        dAW.inc(G.dma_start(out=cmask[:], in_=cmask_in), 16)
        dAW.inc(G.dma_start(out=identb[:], in_=ident_in), 16)
        for nm in ("w_q", "w_k", "w_v", "w_o"):
            dAW.inc(G.dma_start(out=s_w[nm], in_=w_in[nm]), 16)

        WMETA = {"g": ("w_g", 0), "u": ("w_u", 1), "d": ("w_d", 2)}

        def u5_chunk_ap(nm, chi):
            if nm == "w_d":
                r, c = chi // 4, chi % 4
                return (slice(r * 128, (r + 1) * 128),
                        slice(c * 1024, (c + 1) * 1024))
            return (slice(chi * 128, (chi + 1) * 128), slice(0, D))

        def u5_s1(m, lo, hi):
            nm, widx = WMETA[m]
            for chi in range(lo, hi):
                par = chi % 2
                SY.wait_ge(sAB.h, max(0, sAB.v - 1))
                SY.wait_ge(sSG.h, sSG.v)
                dF1[par].inc(SY.dma_start(out=wstg[:, par, :],
                                     in_=w_in[nm][u5_chunk_ap(nm, chi)]), 16)
                SC.wait_ge(dF1[par].h, dF1[par].v)
                sAB.inc(SC.activation(sgA[:, par, :], wstg[:, par, :], AF.Abs,
                                      accum_out=accA[:, widx, chi:chi + 1]), 1)

        def u5_fin(m):
            _, col = WMETA[m]
            T.wait_ge(sAB.h, sAB.v)
            T.wait_ge(sCP.h, sCP.v)
            T.wait_ge(eW3b.h, eW3b.v)
            mMean.inc(T.matmul(psU1[:, 0:32], ones[:], accA[:, col, :],
                               start=True, stop=True), 1)
            V.wait_ge(mMean.h, mMean.v)
            eW3b.inc(V.tensor_reduce(th3[:, 4:5], psU1[:, 0:32], axis=AX.X,
                                     op=A.add), 1)
            SC.wait_ge(eW3b.h, eW3b.v)
            SC.activation(th3[:, 5:6], th3[:, 4:5], AF.Relu,
                          scale=0.5 / (INNER * D), bias=-0.5e-5)
            eTH.inc(SC.activation(th3[:, col:col + 1], th3[:, 5:6], AF.Identity,
                                  bias=0.5e-5), 1)

        def u5T_view(nm, chi):
            if nm == "w_d":
                r, c = chi // 4, chi % 4
                reg = s_wT[nm][c * 1024:(c + 1) * 1024, r * 128:(r + 1) * 128]
            else:
                reg = s_wT[nm][:, chi * 128:(chi + 1) * 128]
            return reg.rearrange("(b p) w -> p b w", b=8)

        w3_ready = {}

        def u5_s2a(m, lo, hi):
            nm, col = WMETA[m]
            for chi in range(lo, hi):
                par = chi % 2
                SY.wait_ge(sSG.h, max(0, sSG.v - 1))
                SY.wait_ge(sAB.h, sAB.v)
                dF2[par].inc(SY.dma_start(out=wstg[:, par, :],
                                     in_=w_in[nm][u5_chunk_ap(nm, chi)]), 16)
                V.wait_ge(dF2[par].h, dF2[par].v)
                V.wait_ge(eTH.h, eTH.v)
                V.wait_ge(eW3.h, max(0, eW3.v - 1))
                V.tensor_scalar(sgA[:, par, :], wstg[:, par, :],
                                th3[:, col:col + 1], 2.0, A.is_ge, A.mult)
                sSG.inc(V.tensor_scalar(sgB[:, par, :], wstg[:, par, :],
                                        th3[:, 5 + col:6 + col], 2.0,
                                        A.is_le, A.mult), 1)
                V.wait_ge(dW3[par].h, dW3[par].v)
                eW3.inc(V.tensor_sub(w3o[:, par, :], sgA[:, par, :],
                                     sgB[:, par, :]), 1)
                w3_ready[(m, chi)] = eW3.v

        def u5_s2b(m, lo, hi):
            nm, col = WMETA[m]
            for chi in range(lo, hi):
                par = chi % 2
                T.wait_ge(eW3.h, w3_ready[(m, chi)])
                T.wait_ge(eWT.h, eWT.v)
                for j in range(NDC):
                    ins = T.matmul(psT[0][:, j * 128:(j + 1) * 128],
                                   w3o[:, par, j * 128:(j + 1) * 128],
                                   identb[:], start=(j == 0), stop=(j == NDC - 1),
                                   is_transpose=True)
                mTW.inc(ins, 1)
                V.wait_ge(mTW.h, mTW.v)
                eWT.inc(V.tensor_copy(wte[:, par, :], psT[0][:]), 1)
                G.wait_ge(eWT.h, eWT.v)
                dW3[par].inc(G.dma_start(
                    out=u5T_view(nm, chi),
                    in_=wte[:, par, :].rearrange("p (b w) -> p b w", b=8)), 16)

        u5_plan = []
        for m in ("g", "u", "d"):
            for j in range(16):
                u5_plan.append(("s1", m, j * 2, (j + 1) * 2))
        s2a = [("s2a", m, j, j + 1) for m in ("g", "u", "d") for j in range(32)]
        s2b = [("s2b", m, j, j + 1) for m in ("g", "u", "d") for j in range(32)]
        for i in range(len(s2a)):
            u5_plan.append(s2a[i])
            if i >= 1:
                u5_plan.append(s2b[i - 1])
        u5_plan.append(s2b[-1])
        u5_pos = [0]

        def u5_pieces(n):
            for _ in range(n):
                if u5_pos[0] >= len(u5_plan):
                    return
                e = u5_plan[u5_pos[0]]
                u5_pos[0] += 1
                if e[0] == "s1":
                    u5_s1(e[1], e[2], e[3])
                elif e[0] == "fin":
                    u5_fin(e[1])
                elif e[0] == "s2a":
                    u5_s2a(e[1], e[2], e[3])
                else:
                    u5_s2b(e[1], e[2], e[3])


        for tc in range(NTC):
            par = tc % 2
            row = slice(tc * 128, (tc + 1) * 128)
            if tc >= 2:
                SY.wait_ge(sLN5.h, tc - 1)
            dX[par].inc(SY.dma_start(out=xld[:, par, :], in_=x_in[row, :]), 16)
            V.wait_ge(dX[par].h, dX[par].v)
            sLN1.inc(V.tensor_reduce(stats[:, tc, 0:1], xld[:, par, :],
                                     axis=AX.X, op=A.add, negate=True), 1)
            SC.wait_ge(sLN1.h, sLN1.v)
            sA1.inc(SC.activation(stats[:, tc, 1:2], stats[:, tc, 0:1],
                                  AF.Identity, scale=1.0 / D), 1)
            V.wait_ge(sA1.h, sA1.v)
            if tc >= 2:
                V.wait_ge(sLN5.h, tc - 1)
            sA2.inc(V.tensor_scalar(xc2[:, par, :], xld[:, par, :],
                                    stats[:, tc, 1:2], None, A.add), 1)
            SC.wait_ge(sA2.h, sA2.v)
            sLN2.inc(SC.activation(xsq[:], xc2[:, par, :], AF.Square), 1)
            V.wait_ge(sLN2.h, sLN2.v)
            V.tensor_reduce(stats[:, tc, 2:3], xsq[:], axis=AX.X, op=A.add)
            sLN3.inc(V.tensor_reduce(stats[:, tc, 3:4], xc2[:, par, :],
                                     axis=AX.X, op=A.max,
                                     apply_absolute_value=True), 1)
            SC.wait_ge(sLN3.h, sLN3.v)
            SC.activation(stats[:, tc, 4:5], stats[:, tc, 2:3], AF.Ln,
                          scale=1.0 / D, bias=EPS_LN)
            sLN4.inc(SC.activation(stats[:, tc, 5:6], stats[:, tc, 4:5],
                                   AF.Exp, scale=-0.5), 1)
            V.wait_ge(sLN4.h, sLN4.v)
            sA3.inc(V.tensor_scalar(stats[:, tc, 6:7], stats[:, tc, 3:4],
                                    stats[:, tc, 5:6], 1e-5, A.mult, A.max), 1)
            SC.wait_ge(sA3.h, sA3.v)
            SC.activation(stats[:, tc, 4:5], stats[:, tc, 6:7], AF.Ln)
            sA4.inc(SC.activation(stats[:, tc, 7:8], stats[:, tc, 4:5],
                                  AF.Exp, scale=-1.0), 1)
            V.wait_ge(sA4.h, sA4.v)
            V.tensor_scalar(stats[:, tc, 8:9], stats[:, tc, 7:8],
                            stats[:, tc, 5:6], 127.0, A.mult, A.mult)
            V.tensor_scalar(stats[:, tc, 9:10], stats[:, tc, 6:7], 1.0 / 127.0,
                            None, A.mult)
            if tc >= 2:
                V.wait_ge(dXN[par].h, dXN[par].v)
            V.tensor_scalar(stgn[:, par, :], xc2[:, par, :], stats[:, tc, 5:6],
                            None, A.mult)
            _stair(nc, stgq[:, par, :], xc2[:, par, :], stats[:, tc, 8:9],
                   rO, rA, cbf, D, done_sem=sLN5)
            G.wait_ge(sLN5.h, sLN5.v)
            dXN[par].inc(G.dma_start(out=s_xn[row, :], in_=stgn[:, par, :]), 16)
            dXN[par].inc(G.dma_start(out=s_xq[row, :], in_=stgq[:, par, :]), 16)
            u5_pieces(3)

        for m in ("g", "u", "d"):
            u5_fin(m)
        for i in range(3):
            ins = SC.activation(th3[:, 5 + i:6 + i], th3[:, i:i + 1],
                                AF.Identity, scale=-1.0)
        eTH.inc(ins, 1)
        T.wait_ge(eW3b.h, eW3b.v)

        probe_stop("u0")
        dAW.inc(G.dma_start(out=indt[0:16, :, :], in_=indt_in), 16)
        SY.wait_ge(dXN[0].h, dXN[0].v)
        SY.wait_ge(dXN[1].h, dXN[1].v)
        for dc in range(NDC):
            dXNL.inc(SY.dma_start(out=xn_fm[:, dc, :],
                                  in_=s_xn[:, dc * 128:(dc + 1) * 128],
                                  transpose=True), 16)
        ctxLN.close()

        q_fm = sb(ctxQ, "q_fm", [128, NDC, S], BF16)
        k_fm = sb(ctxQ, "k_fm", [128, NDC, S], BF16)
        v_ext = sb(ctxQ, "v_ext", [128, NTC, 16, 65], BF16)
        V.memset(v_ext[:], 1.0)

        evac_gate = {id(psU1): (0, 0), id(psU2): (0, 0)}

        def proj_qk(wscr, dst_fm):
            SY.wait_ge(dAW.h, dAW.v)
            SY.wait_ge(mPJ.h, mPJ.v)
            for dc in range(NDC):
                dWT.inc(SY.dma_start(out=wT[:, dc, :],
                                     in_=wscr[:, dc * 128:(dc + 1) * 128],
                                     transpose=True), 16)
            probe_stop("uqw")
            for ts in range(4):
                tsl = slice(ts * 512, (ts + 1) * 512)
                for oc in range(NDC):
                    ps = psU1 if oc % 2 == 0 else psU2
                    g1, g2 = evac_gate[id(ps)]
                    T.wait_ge(sSQ.h, g1)
                    T.wait_ge(sCP.h, g2)
                    T.wait_ge(dWT.h, dWT.v)
                    T.wait_ge(dXNL.h, dXNL.v)
                    for dc in range(NDC):
                        ins = T.matmul(ps[:], wT[:, dc, oc * 128:(oc + 1) * 128],
                                       xn_fm[:, dc, tsl], start=(dc == 0),
                                       stop=(dc == NDC - 1))
                    mPJ.inc(ins, 1)
                    par2 = oc % 2
                    SC.wait_ge(mPJ.h, mPJ.v)
                    sSQ.inc(SC.activation(qsq[:, par2, :], ps[:], AF.Square), 1)
                    V.wait_ge(sSQ.h, sSQ.v)
                    sCP.inc(V.tensor_copy(dst_fm[:, oc, tsl], ps[:]), 1)
                    evac_gate[id(ps)] = (sSQ.v, sCP.v)
                    probe_stop("uq0")
                    if proj_mode >= 2:
                        continue
                    T.wait_ge(sSQ.h, sSQ.v)
                    if oc == 0:
                        T.wait_ge(sA.h, sA.v)
                    ins2 = T.matmul(psSS[:], ind[:, oc, :], qsq[:, par2, :],
                                    start=(oc == 0), stop=(oc == NDC - 1))
                if proj_mode >= 2:
                    continue
                mSS.inc(ins2, 1)
                SC.wait_ge(mSS.h, mSS.v)
                SC.activation(a_ln[:], psSS[:], AF.Ln, scale=1.0 / DK,
                              bias=EPS_RMS)
                sA.inc(SC.activation(a_q[:, tsl], a_ln[:], AF.Exp, scale=-0.5), 1)
                if proj_mode >= 1:
                    continue
                for oc in range(NDC):
                    T.wait_ge(sA.h, sA.v)
                    T.wait_ge(eRMS.h, eRMS.v)
                    mBC.inc(T.matmul(psY[:], indt[0:16, oc, :], a_q[0:16, tsl],
                                     start=True, stop=True), 1)
                    V.wait_ge(mBC.h, mBC.v)
                    eRMS.inc(V.tensor_mul(dst_fm[:, oc, tsl], dst_fm[:, oc, tsl],
                                          psY[:]), 1)

        proj_qk(s_w["w_q"], q_fm)
        probe_stop("uq")
        proj_qk(s_w["w_k"], k_fm)

        SY.wait_ge(mPJ.h, mPJ.v)
        for dc in range(NDC):
            dWT.inc(SY.dma_start(out=wT[:, dc, :],
                                 in_=s_w["w_v"][:, dc * 128:(dc + 1) * 128],
                                 transpose=True), 16)
        for tc in range(NTC):
            for vs in range(2):
                ps = psU1 if (tc * 2 + vs) % 2 == 0 else psU2
                g1, g2 = evac_gate[id(ps)]
                T.wait_ge(sCP.h, g2)
                T.wait_ge(dWT.h, dWT.v)
                for dc in range(NDC):
                    ins = T.matmul(ps[:], xn_fm[:, dc, tc * 128:(tc + 1) * 128],
                                   wT[:, dc, vs * 512:(vs + 1) * 512],
                                   start=(dc == 0), stop=(dc == NDC - 1))
                mPJ.inc(ins, 1)
                V.wait_ge(mPJ.h, mPJ.v)
                sCP.inc(V.tensor_copy(
                    v_ext[:, tc, vs * 8:(vs + 1) * 8, 0:64],
                    ps[:].rearrange("p (h f) -> p h f", h=8)), 1)
                evac_gate[id(ps)] = (sSQ.v, sCP.v)

        ctxPJ.close()
        ctxPSS.close()
        psT[0] = ctx.enter_context(nc.psum_tensor("psT", [128, 1024], BF16))
        probe_stop("u3")
        y_tm = sbr(ctxB, "y_tm", [128, NTC, D], BF16)
        t1b = sbr(ctxB, "t1b", [128, 2, 1024], F32)
        p_sb = sbr(ctxB, "p_sb", [128, 2, 1024], BF16)
        den = sbr(ctxB, "den", [128, 4], F32)

        T.wait_ge(eRMS.h, eRMS.v)
        T.wait_ge(sCP.h, sCP.v)
        for dnm, dbuf in (("q_fm", q_fm), ("k_fm", k_fm), ("v_ext", v_ext)):
            if dnm in dbg:
                G.wait_ge(eRMS.h, eRMS.v)
                G.wait_ge(sCP.h, sCP.v)
                dAW.inc(G.dma_start(out=dbg[dnm], in_=dbuf[:]), 16)

        pair_idx = [0]
        sc_gate = [0, 0]
        p_gate = [0, 0]

        for h in range(H):
            oc = h // 2
            pr = (h % 2) * 64
            for qt in range(4):
                qsl = slice(qt * 512, (qt + 1) * 512)
                T.wait_ge(sYN.h, h * 4 + qt)
                nr = 4 * qt + 4
                for j in range(nr // 2):
                    r0, r1 = 2 * j, 2 * j + 1
                    par = pair_idx[0] % 2
                    ps = psSCa if par == 0 else psSCb
                    T.wait_ge(sTNH.h, sc_gate[par])
                    for rr, half in ((r0, 0), (r1, 1)):
                        ins = T.matmul(ps[:, half * 512:(half + 1) * 512],
                                       k_fm[pr:pr + 64, oc, rr * 128:(rr + 1) * 128],
                                       q_fm[pr:pr + 64, oc, qsl],
                                       start=True, stop=True)
                    mQK.inc(ins, 1)
                    SC.wait_ge(mQK.h, mQK.v)
                    SC.wait_ge(mPV.h, p_gate[par])
                    sTNH.inc(SC.activation(t1b[:, par, :], ps[:], AF.Tanh,
                                           scale=1.0 / (8.0 * SOFTCAP)), 1)
                    sc_gate[par] = sTNH.v
                    sEXP.inc(SC.activation(p_sb[:, par, :], t1b[:, par, :],
                                           AF.Exp, scale=SOFTCAP, bias=-8.0), 1)
                    nmask = 0
                    for rr, half in ((r0, 0), (r1, 1)):
                        if 4 * qt <= rr:
                            V.wait_ge(sEXP.h, sEXP.v)
                            sl = half * 512 + (rr - 4 * qt) * 128
                            eMSK.inc(V.tensor_mul(p_sb[:, par, sl:sl + 128],
                                                  p_sb[:, par, sl:sl + 128],
                                                  cmask[:]), 1)
                            nmask += 1
                    T.wait_ge(sEXP.h, sEXP.v)
                    if nmask:
                        T.wait_ge(eMSK.h, eMSK.v)
                    for rr, half in ((r0, 0), (r1, 1)):
                        for cl in range(4):
                            c = 4 * qt + cl
                            if c < rr:
                                continue
                            ins = T.matmul(psY[:, cl * 128:cl * 128 + 65],
                                           p_sb[:, par, half * 512 + cl * 128:
                                                half * 512 + (cl + 1) * 128],
                                           v_ext[:, rr, h, :],
                                           start=(rr == 0 and cl == 0),
                                           stop=(rr == 4 * qt + 3 and cl == 3))
                    mPV.inc(ins, 1)
                    p_gate[par] = mPV.v
                    pair_idx[0] += 1
                V.wait_ge(mPV.h, mPV.v)
                for cl in range(4):
                    ins = V.reciprocal(den[:, cl:cl + 1],
                                       psY[:, cl * 128 + 64:cl * 128 + 65])
                eREC.inc(ins, 1)
                SC.wait_ge(eREC.h, eREC.v)
                for cl in range(4):
                    ins = SC.activation(y_tm[:, 4 * qt + cl, h * 64:(h + 1) * 64],
                                        psY[:, cl * 128:cl * 128 + 64],
                                        AF.Identity, scale=den[:, cl:cl + 1])
                sYN.inc(ins, 1)
                u5_pieces(3)
        u5_pieces(100)

        probe_stop("u4")
        ctxQ.close()
        ctxU5.close()

        y_fm = sb(ctxC, "y_fm", [128, NDC, S], BF16)
        woT = sb(ctxC, "woT", [128, NDC, D], BF16)
        x_re = sb(ctxC, "x_re", [128, 2, D], F32)
        res_sb = sb(ctxC, "res_sb", [128, 2, D], F32)

        G.wait_ge(sYN.h, sYN.v)
        if "y_tm" in dbg:
            dAW.inc(G.dma_start(out=dbg["y_tm"], in_=y_tm[:]), 16)
        for tc in range(NTC):
            dY.inc(G.dma_start(out=s_y[tc * 128:(tc + 1) * 128, :],
                               in_=y_tm[:, tc, :]), 16)
        SY.wait_ge(dY.h, dY.v)
        SY.wait_ge(dW3[0].h, dW3[0].v)
        SY.wait_ge(dW3[1].h, dW3[1].v)
        for dc in range(NDC):
            dYL.inc(SY.dma_start(out=y_fm[:, dc, :],
                                 in_=s_y[:, dc * 128:(dc + 1) * 128],
                                 transpose=True), 16)
            dWO.inc(SY.dma_start(out=woT[:, dc, :],
                                 in_=s_w["w_o"][:, dc * 128:(dc + 1) * 128],
                                 transpose=True), 16)
        T.wait_ge(eTH.h, eTH.v)
        res_gate = {id(psU1): 0, id(psU2): 0}
        for tc in range(NTC):
            par = tc % 2
            row = slice(tc * 128, (tc + 1) * 128)
            if tc >= 2:
                SY.wait_ge(eRS.h, 2 * (tc - 1))
            dXR[par].inc(SY.dma_start(out=x_re[:, par, :], in_=x_in[row, :]), 16)
            for os_ in range(2):
                ps = psU1 if os_ == 0 else psU2
                T.wait_ge(eRS.h, res_gate[id(ps)])
                T.wait_ge(dYL.h, dYL.v)
                T.wait_ge(dWO.h, dWO.v)
                for dc in range(NDC):
                    ins = T.matmul(ps[:], y_fm[:, dc, tc * 128:(tc + 1) * 128],
                                   woT[:, dc, os_ * 512:(os_ + 1) * 512],
                                   start=(dc == 0), stop=(dc == NDC - 1))
                mO.inc(ins, 1)
                V.wait_ge(mO.h, mO.v)
                V.wait_ge(dXR[par].h, dXR[par].v)
                if tc >= 2 and os_ == 0:
                    V.wait_ge(dRS[par].h, dRS[par].v)
                eRS.inc(V.tensor_add(res_sb[:, par, os_ * 512:(os_ + 1) * 512],
                                     ps[:],
                                     x_re[:, par, os_ * 512:(os_ + 1) * 512]), 1)
                res_gate[id(ps)] = eRS.v
            G.wait_ge(eRS.h, eRS.v)
            dRS[par].inc(G.dma_start(out=s_res[row, :], in_=res_sb[:, par, :]), 16)
        ctxB.close()
        ctxC.close()

        probe_stop("u8")
        xqg = sb(ctxF, "xqg", [128, 2, NDC, 512], BF16)
        gusl = sb(ctxF, "gusl", [128, 2, 2, NDC, 512], BF16)
        gate_sb = sb(ctxF, "gate_sb", [128, 512], F32)
        h_sb = sb(ctxF, "h_sb", [128, 2, 512], F32)
        hload = sb(ctxF, "hload", [128, 2, INNER], F32)
        r7O = sb(ctxF, "r7O", [128, INNER // 4], F32)
        r7A = sb(ctxF, "r7A", [128, INNER // 4], F32)
        c7s = sb(ctxF, "c7s", [128, 512], F32)
        hqq = sb(ctxF, "hqq", [128, 2, 1024], BF16)
        wdT = sb(ctxF, "wdT", [128, NICD, D], BF16)
        hqT = sb(ctxF, "hqT", [128, 4, NICD, 128], BF16)
        res_ld = sb(ctxF, "res_ld", [128, 1, D], F32)
        out_sb = sb(ctxF, "out_sb", [128, 1, D], F32)

        V.wait_ge(eTH.h, eTH.v)
        for tc in range(NTC):
            V.tensor_scalar(sc2[:, tc, 2:3], stats[:, tc, 9:10],
                            th3[:, 0:1], None, A.mult)
            V.tensor_scalar(sc2[:, tc, 3:4], stats[:, tc, 9:10],
                            th3[:, 1:2], None, A.mult)
        SY.wait_ge(sYN.h, 4 * H)
        SY.wait_ge(dY.h, dY.v)
        SY.wait_ge(dRS[0].h, dRS[0].v)
        SY.wait_ge(dRS[1].h, dRS[1].v)
        SY.wait_ge(dW3[0].h, dW3[0].v)
        SY.wait_ge(dW3[1].h, dW3[1].v)
        T.wait_ge(eRS.h, eRS.v)
        T.wait_ge(sTNH.h, sTNH.v)
        gu_gate = [0, 0]
        xq_gate = [0, 0]
        psg1 = [0, 0]
        psg2 = [0, 0]
        dn_gate = [0, 0]
        psDN = (psSCa[:, 0:512], psSCa[:, 512:1024])
        psG = (psU1[:], psSCb[:, 0:512])
        psU = (psU2[:], psY[:])
        dXQg = [dXQL, dYL]

        def drain_stair(n=1):
            for _ in range(n):
                if stair_q:
                    stair_q.pop(0)()

        def u6_group(grp, drain=False):
            xbuf = grp % 2
            SY.wait_ge(mGU.h, xq_gate[xbuf])
            for dc in range(NDC):
                dXQg[xbuf].inc(SY.dma_start(
                    out=xqg[:, xbuf, dc, :],
                    in_=s_xq[grp * 512:(grp + 1) * 512,
                             dc * 128:(dc + 1) * 128],
                    transpose=True), 16)
            for ic in range(NIC):
                buf = (grp * NIC + ic) % 2
                SY.wait_ge(mGU.h, gu_gate[buf])
                for w, nm in ((0, "w_g"), (1, "w_u")):
                    dGU[buf].inc(SY.dma_start(
                        out=gusl[:, w, buf, :, :],
                        in_=s_wT[nm][:, ic * 512:(ic + 1) * 512]
                            .rearrange("(b p) c -> p b c", b=NDC)), 16)
                for tl in range(4):
                    tc = grp * 4 + tl
                    unit = grp * 32 + ic * 4 + tl
                    ub = unit % 2
                    T.wait_ge(dGU[buf].h, dGU[buf].v)
                    T.wait_ge(dXQg[xbuf].h, dXQg[xbuf].v)
                    T.wait_ge(sSIL.h, psg1[ub])
                    for dc in range(NDC):
                        T.matmul(psG[ub],
                                 xqg[:, xbuf, dc, tl * 128:(tl + 1) * 128],
                                 gusl[:, 0, buf, dc, :], start=(dc == 0),
                                 stop=(dc == NDC - 1))
                    T.wait_ge(eH.h, psg2[ub])
                    for dc in range(NDC):
                        ins = T.matmul(psU[ub],
                                       xqg[:, xbuf, dc, tl * 128:(tl + 1) * 128],
                                       gusl[:, 1, buf, dc, :], start=(dc == 0),
                                       stop=(dc == NDC - 1))
                    mGU.inc(ins, 1)
                    SC.wait_ge(mGU.h, mGU.v)
                    SC.wait_ge(eH.h, eH.v)
                    sSIL.inc(SC.activation(gate_sb[:], psG[ub], AF.Silu,
                                           scale=sc2[:, tc, 2:3]), 1)
                    psg1[ub] = sSIL.v
                    hpar = unit % 2
                    V.wait_ge(sSIL.h, sSIL.v)
                    V.wait_ge(dH[hpar].h, dH[hpar].v)
                    eH.inc(V.scalar_tensor_tensor(h_sb[:, hpar, :], psU[ub],
                                                  sc2[:, tc, 3:4], gate_sb[:],
                                                  A.mult, A.mult), 1)
                    psg2[ub] = eH.v
                    G.wait_ge(eH.h, eH.v)
                    dH[hpar].inc(G.dma_start(
                        out=s_h[tc * 128:(tc + 1) * 128,
                                ic * 512:(ic + 1) * 512],
                        in_=h_sb[:, hpar, :]), 16)
                gu_gate[buf] = mGU.v
                if drain:
                    drain_stair(3)
            xq_gate[xbuf] = mGU.v

        def mk_stair_pieces(grp):
            snap0, snap1 = dH[0].v, dH[1].v
            qw = INNER // 4

            def pro(tc):
                row = slice(tc * 128, (tc + 1) * 128)
                SC.wait_ge(dH[0].h, snap0)
                SC.wait_ge(dH[1].h, snap1)
                SC.wait_ge(eSTR.h, max(0, eSTR.v - 4))
                dHL.inc(SC.dma_start(out=hload[:, tc % 2, :],
                                     in_=s_h[row, :]), 16)

            def piece_a(tc, first):
                pro(tc)
                V.wait_ge(dHL.h, dHL.v)
                eRM.inc(V.tensor_reduce(rmax[:, 0, tc:tc + 1],
                                        hload[:, tc % 2, :],
                                        axis=AX.X, op=A.max,
                                        apply_absolute_value=True), 1)
                SC.wait_ge(eRM.h, eRM.v)
                SC.activation(rmax[:, 1, tc:tc + 1], rmax[:, 0, tc:tc + 1],
                              AF.Relu, bias=-1e-5)
                eCL.inc(SC.activation(rmax[:, 0, tc:tc + 1],
                                      rmax[:, 1, tc:tc + 1], AF.Identity,
                                      bias=EPS_LN), 1)
                V.wait_ge(eCL.h, eCL.v)
                eRC.inc(V.reciprocal(rmax[:, 1, tc:tc + 1],
                                     rmax[:, 0, tc:tc + 1]), 1)
                V.tensor_scalar(sc2[:, tc, 0:1], rmax[:, 0, tc:tc + 1],
                                th3[:, 2:3], 1.0 / 127.0, A.mult,
                                A.mult)
                SC.wait_ge(eRC.h, eRC.v)
                eAS.inc(SC.activation(sc2[:, tc, 1:2], rmax[:, 1, tc:tc + 1],
                                      AF.Identity, scale=127.0), 1)
                V.wait_ge(eAS.h, eAS.v)
                if tc >= 1:
                    pq = (tc - 1) // 4
                    V.wait_ge(dHQq[pq].h, dHQq[pq].v)

            def quarter(tc, qi):
                tl = tc % 4
                qb = (tc * 4 + qi) % 2

                def hsl(a, b):
                    return hload[:, tc % 2, qi * qw + a:qi * qw + b]
                V.wait_ge(eAS.h, eAS.v)
                V.wait_ge(mTR.h, max(0, mTR.v - 1))
                _stair(nc, hqq[:, qb, 0:512], hsl(0, 512),
                       sc2[:, tc, 1:2], r7O, r7A, c7s, 512)
                _stair(nc, hqq[:, qb, 512:1024], hsl(512, 1024),
                       sc2[:, tc, 1:2], r7O, r7A, c7s, 512,
                       done_sem=eSTR)
                T.wait_ge(eSTR.h, eSTR.v)
                T.wait_ge(eTR.h, eTR.v)
                for j in range(8):
                    ins = T.matmul(psT[0][:, j * 128:(j + 1) * 128],
                                   hqq[:, qb, j * 128:(j + 1) * 128],
                                   identb[:], start=(j == 0),
                                   stop=(j == 7), is_transpose=True)
                mTR.inc(ins, 1)
                SC.wait_ge(mTR.h, mTR.v)
                eTR.inc(SC.activation(hqT[:, tl, qi * 8:(qi + 1) * 8, :],
                                      psT[0][:], AF.Identity), 1)
                if qi == 3:
                    tr_done[tc] = eTR.v

            for tl in range(4):
                tc = grp * 4 + tl
                stair_q.append(lambda tc=tc, tl=tl: piece_a(tc, tl == 0))
                for qi in range(4):
                    stair_q.append(lambda tc=tc, qi=qi: quarter(tc, qi))

        def down_group(g):
            for tl in range(4):
                tc = g * 4 + tl
                par = tc % 2
                row = slice(tc * 128, (tc + 1) * 128)
                if tc >= 1:
                    SY.wait_ge(eOUT.h, 2 * tc)
                dRL[par].inc(SY.dma_start(out=res_ld[:, 0, :],
                                          in_=s_res[row, :]), 16)
                for os_ in range(2):
                    ps = psDN[os_]
                    T.wait_ge(eOUT.h, dn_gate[os_])
                    T.wait_ge(eTR.h, tr_done[tc])
                    T.wait_ge(dWD.h, dWD.v)
                    for ic in range(NICD):
                        ins = T.matmul(
                            ps,
                            hqT[:, tl, ic, :],
                            wdT[:, ic, os_ * 512:(os_ + 1) * 512],
                            start=(ic == 0), stop=(ic == NICD - 1))
                    mDN.inc(ins, 1)
                    V.wait_ge(mDN.h, mDN.v)
                    V.wait_ge(dRL[par].h, dRL[par].v)
                    if tc >= 1 and os_ == 0:
                        V.wait_ge(dOUT[0].h, dOUT[0].v)
                    eOUT.inc(V.scalar_tensor_tensor(
                        out_sb[:, 0, os_ * 512:(os_ + 1) * 512], ps,
                        sc2[:, tc, 0:1],
                        res_ld[:, 0, os_ * 512:(os_ + 1) * 512],
                        A.mult, A.add), 1)
                    dn_gate[os_] = eOUT.v
                SC.wait_ge(eOUT.h, eOUT.v)
                dOUT[0].inc(SC.dma_start(out=out_hbm[row, :],
                                         in_=out_sb[:, 0, :]), 16)
                drain_stair(1)

        stair_q = []
        tr_done = {}
        u6_group(0)
        dWD.inc(SY.dma_start(out=wdT[:],
                             in_=s_wT["w_d"]
                                 .rearrange("(b p) c -> p b c", b=NICD)), 16)
        mk_stair_pieces(0)
        for grp in range(1, 4):
            u6_group(grp, drain=True)
            drain_stair(20)
            down_group(grp - 1)
            mk_stair_pieces(grp)
        drain_stair(20)
        down_group(3)
        drain_stair(20)

        G.wait_ge(dOUT[0].h, dOUT[0].v)
        G.wait_ge(dOUT[1].h, dOUT[1].v)
        scr_map = {"s_xn": s_xn, "s_xq": s_xq, "s_y": s_y, "s_h": s_h,
                   "s_hq": s_hq, "s_res": s_res, "s_w_g": s_w["w_g"],
                   "s_w_u": s_w["w_u"], "s_w_d": s_w["w_d"],
                   "th3": th3[:], "sc2": sc2[:]}
        ndbg = 0
        for dnm, src in scr_map.items():
            if dnm in dbg:
                dAW.inc(G.dma_start(out=dbg[dnm], in_=src), 16)
                ndbg += 1
        if ndbg:
            G.wait_ge(dAW.h, dAW.v)
        ctxG.close()

    @blk.sync
    def _(_sync):
        try:
            _body(_sync)
        except _Done:
            pass
        for st in (ctxG, ctxF, ctxC, ctxB, ctxQ, ctxU5, ctxPJ, ctxLN):
            st.close()

    ctx.close()
    return nc


def _host_consts():
    ind = np.zeros((NDC, 128, 16), dtype=ml_dtypes.bfloat16)
    indt = np.zeros((16, NDC, 128), dtype=np.float32)
    for dc in range(NDC):
        for p in range(128):
            hcol = 2 * dc + p // 64
            ind[dc, p, hcol] = 1
            indt[hcol, dc, p] = 1.0
    ones = np.ones((128, 128), dtype=np.float32)
    cmask = np.triu(np.ones((128, 128))).astype(ml_dtypes.bfloat16)
    ident = np.eye(128).astype(ml_dtypes.bfloat16)
    consts = np.zeros((128, 16), dtype=np.float32)
    for i, vv in enumerate(CONST_VALS):
        consts[:, i] = vv
    return ind, indt, ones, cmask, ident, consts


def kernel(L0_attn_x, L0_attn_w_q, L0_attn_w_k, L0_attn_w_v, L0_attn_w_o,
           L0_ffn_w_gate, L0_ffn_w_up, L0_ffn_w_down):
    nc = build_nc()
    ind, indt, ones, cmask, ident, consts = _host_consts()
    x = np.ascontiguousarray(np.asarray(L0_attn_x, dtype=np.float32))
    shared = {
        "w_q": np.ascontiguousarray(L0_attn_w_q, dtype=np.float32),
        "w_k": np.ascontiguousarray(L0_attn_w_k, dtype=np.float32),
        "w_v": np.ascontiguousarray(L0_attn_w_v, dtype=np.float32),
        "w_o": np.ascontiguousarray(L0_attn_w_o, dtype=np.float32),
        "w_g": np.ascontiguousarray(L0_ffn_w_gate, dtype=np.float32),
        "w_u": np.ascontiguousarray(L0_ffn_w_up, dtype=np.float32),
        "w_d": np.ascontiguousarray(L0_ffn_w_down, dtype=np.float32),
        "ind": ind, "indt": indt, "ones": ones, "cmask": cmask,
        "ident": ident,
        "consts": consts,
    }
    in_maps = [{"x": np.ascontiguousarray(x[i]), **shared} for i in range(8)]
    res = run_bass_kernel_spmd(nc, in_maps, core_ids=list(range(8)))
    return np.stack([res.results[i]["out"] for i in range(8)], axis=0)
